# version 01
# baseline (speedup 1.0000x reference)
import math
from contextlib import ExitStack

import numpy as np

import concourse.bass as bass
import concourse.mybir as mybir
from concourse.alu_op_type import AluOpType
from concourse.bass_utils import run_bass_kernel_spmd

F32 = mybir.dt.float32
BF16 = mybir.dt.bfloat16
I32 = mybir.dt.int32
AF = mybir.ActivationFunctionType
AX = mybir.AxisListType
OP = AluOpType

N_CORES = 8
B, S, D = 4, 4096, 1024
T_LOC = S // 2
P = 128


ENGINES = ("sp", "act", "ve", "te", "pool")


class Buf:

    __slots__ = ("h", "w", "r", "name")

    def __init__(self, h, name):
        self.h = h
        self.name = name
        self.w = None
        self.r = []

    def ap(self):
        return self.h[:, :]

    def __getitem__(self, idx):
        return self.h[idx]


class Prog:
    def __init__(self, nc, stack):
        self.nc = nc
        self.stack = stack
        self.ops = {e: [] for e in ENGINES}
        self.cnt = {e: 0 for e in ENGINES}
        self.sems = {
            e: stack.enter_context(nc.semaphore(f"sem_{e}")) for e in ENGINES
        }
        self.waited = {e: {p: 0 for p in ENGINES} for e in ENGINES}
        self._bufn = 0

    def sbuf(self, name, shape, dtype):
        st = self._scopes[-1] if getattr(self, "_scopes", None) else self.stack
        h = st.enter_context(self.nc.sbuf_tensor(name, list(shape), dtype))
        return Buf(h, name)

    def open_scope(self):
        if not hasattr(self, "_scopes"):
            self._scopes = []
        self._scopes.append(ExitStack())

    def close_scope(self):
        self.barrier()
        self._scopes.pop().close()

    def psum(self, name, shape, dtype=F32):
        h = self.stack.enter_context(self.nc.psum_tensor(name, list(shape), dtype))
        return Buf(h, name)

    def dram(self, name, shape, dtype):
        h = self.nc.dram_tensor(name, list(shape), dtype)
        return Buf(h, name)

    def emit(self, eng, fn, reads=(), writes=(), step=1):
        need = {}

        def merge(ev):
            if ev is None:
                return
            pe, v = ev
            if need.get(pe, 0) < v:
                need[pe] = v

        for b in reads:
            merge(b.w)
        for b in writes:
            merge(b.w)
            for ev in b.r:
                merge(ev)

        waits = []
        wt = self.waited[eng]
        for pe, v in need.items():
            if pe == eng:
                pass
            if wt[pe] < v:
                waits.append((pe, v))
                wt[pe] = v

        ev = (eng, self.cnt[eng] + step)
        self.cnt[eng] = ev[1]
        sems = self.sems
        inc_sem = sems[eng]

        def thunk(h):
            for pe, v in waits:
                h.wait_ge(sems[pe], v)
            last = fn(h)
            last.then_inc(inc_sem, step)

        self.ops[eng].append(thunk)

        for b in reads:
            b.r.append(ev)
        for b in writes:
            b.w = ev
            b.r = []
        return ev

    def dma(self, eng, out, in_, reads=(), writes=(), **kw):
        return self.emit(
            eng, lambda h: h.dma_start(out=out, in_=in_, **kw),
            reads=reads, writes=writes, step=16,
        )

    def barrier(self, engines=ENGINES):
        evs = [(e, self.cnt[e]) for e in ENGINES if self.cnt[e] > 0]
        for eng in engines:
            wt = self.waited[eng]
            waits = [(pe, v) for pe, v in evs if wt[pe] < v and pe != eng]
            for pe, v in waits:
                wt[pe] = v
            if not waits:
                continue
            sems = self.sems

            def thunk(h, waits=waits):
                for pe, v in waits:
                    h.wait_ge(sems[pe], v)

            self.ops[eng].append(thunk)

    def finalize(self):
        nc = self.nc
        with nc.Block() as block:
            for eng_name, dec in (
                ("sp", block.sync),
                ("act", block.scalar),
                ("ve", block.vector),
                ("te", block.tensor),
                ("pool", block.gpsimd),
            ):
                ops = self.ops[eng_name]
                if not ops:
                    continue

                def run(h, ops=ops):
                    for f in ops:
                        f(h)

                dec(run)



def ceil_div(a, b):
    return (a + b - 1) // b


class Common:

    def __init__(self, p, n_psum_f32=4, ln_cols=1024):
        self.p = p
        self.ident = p.sbuf("c_ident", [P, P], BF16)
        iorow = p.sbuf("c_iorow", [P, P], F32)
        iocol = p.sbuf("c_iocol", [P, 1], F32)
        p.emit("pool", lambda h: h.iota(iorow.h[:, :], pattern=[[1, P]], base=0,
                                        channel_multiplier=0, allow_small_or_imprecise_dtypes=True),
               writes=[iorow])
        p.emit("pool", lambda h: h.iota(iocol.h[:, :], pattern=[[0, 1]], base=0,
                                        channel_multiplier=1, allow_small_or_imprecise_dtypes=True),
               writes=[iocol])
        p.emit("ve", lambda h: h.tensor_scalar(self.ident.h[:, :], iorow.h[:, :],
                                               iocol.h[:, 0:1], None, OP.is_equal),
               reads=[iorow, iocol], writes=[self.ident])
        self.iocol = iocol
        self.ones_row = p.sbuf("c_ones", [1, P], BF16)
        p.emit("pool", lambda h: h.memset(self.ones_row.h[:, :], 1.0), writes=[self.ones_row])
        self.ones_row_f = p.sbuf("c_onesf", [1, P], F32)
        p.emit("pool", lambda h: h.memset(self.ones_row_f.h[:, :], 1.0), writes=[self.ones_row_f])
        self.ones_col_f = p.sbuf("c_onescf", [P, 1], F32)
        p.emit("pool", lambda h: h.memset(self.ones_col_f.h[:, :], 1.0), writes=[self.ones_col_f])
        self.ps = [p.psum(f"c_ps{i}", [P, 512], F32) for i in range(n_psum_f32)]
        self._ps_i = 0
        self.ps_tr = p.psum("c_pstr", [P, 512], BF16)
        self.sq = p.sbuf("c_sq", [P, ln_cols], F32)
        self.mu = p.sbuf("c_mu", [P, 1], F32)
        self.ssq = p.sbuf("c_ssq", [P, 1], F32)
        self.var = p.sbuf("c_var", [P, 1], F32)
        self.rstd = p.sbuf("c_rstd", [P, 1], F32)

        self._ccols = {}
        self.ident_f = None
        self.bias_stage = None

    def need_identf(self):
        if self.ident_f is None:
            p = self.p
            self.ident_f = p.sbuf("c_identf", [P, P], F32)
            p.emit("ve", lambda h: h.tensor_copy(self.ident_f.h[:, :], self.ident.h[:, :]),
                   reads=[self.ident], writes=[self.ident_f])
            self.bias_stage = p.sbuf("c_bstage", [P, P], F32)

    def psum_next(self):
        b = self.ps[self._ps_i % len(self.ps)]
        self._ps_i += 1
        return b

    def constcol(self, value, dtype=F32):
        key = (float(value), dtype)
        if key not in self._ccols:
            i = len(self._ccols)
            b = self.p.sbuf(f"c_const{i}", [P, 1], dtype)
            self.p.emit("pool", lambda h: h.memset(b.h[:, :], value), writes=[b])
            self._ccols[key] = b
        return self._ccols[key]


def emit_ln(p, c, x_buf, x_ap, out_buf, out_ap, cols=1024):
    inv = 1.0 / cols
    p.emit("ve", lambda h: h.tensor_reduce(c.mu.h[:, :], x_ap, axis=AX.X, op=OP.add),
           reads=[x_buf], writes=[c.mu])
    p.emit("act", lambda h: h.activation(c.sq.h[:, 0:cols], x_ap, AF.Square,
                                         accum_out=c.ssq.h[:, 0:1]),
           reads=[x_buf], writes=[c.sq, c.ssq])
    p.emit("ve", lambda h: h.tensor_scalar(c.mu.h[:, :], c.mu.h[:, :], inv, None, OP.mult),
           reads=[c.mu], writes=[c.mu])
    p.emit("ve", lambda h: h.tensor_tensor(c.var.h[:, :], c.mu.h[:, :], c.mu.h[:, :], OP.mult),
           reads=[c.mu], writes=[c.var])
    p.emit("ve", lambda h: h.scalar_tensor_tensor(c.var.h[:, :], c.ssq.h[:, :], inv,
                                                  c.var.h[:, :], OP.mult, OP.subtract),
           reads=[c.ssq, c.var], writes=[c.var])
    epsc = c.constcol(1e-5)
    p.emit("act", lambda h: h.activation(c.rstd.h[:, :], c.var.h[:, :], AF.Sqrt,
                                         bias=epsc.h[:, 0:1]),
           reads=[c.var, epsc], writes=[c.rstd])
    p.emit("ve", lambda h: h.reciprocal(c.rstd.h[:, :], c.rstd.h[:, :]),
           reads=[c.rstd], writes=[c.rstd])
    p.emit("ve", lambda h: h.tensor_scalar(out_ap, x_ap, c.mu.h[:, 0:1], c.rstd.h[:, 0:1],
                                           OP.subtract, OP.mult),
           reads=[x_buf, c.mu, c.rstd], writes=[out_buf])


def emit_transpose4(p, c, srcs, dsts, src_bufs, dst_bufs):
    n = len(srcs)
    assert n <= 4

    def ftr(h):
        last = None
        for i, s in enumerate(srcs):
            last = h.transpose(c.ps_tr.h[:, i * P:(i + 1) * P], s, c.ident.h[:, :])
        return last

    p.emit("te", ftr, reads=list(src_bufs) + [c.ident], writes=[c.ps_tr])

    def fev(h):
        last = None
        for i, d in enumerate(dsts):
            last = h.tensor_copy(d, c.ps_tr.h[:, i * P:(i + 1) * P])
        return last

    p.emit("ve", fev, reads=[c.ps_tr], writes=list(dst_bufs))


def emit_cols_from_1d(p, c, src_ap_2d, rows, dst_buf, dst_ap):
    assert rows <= P
    p.dma("sp", c.bias_stage.h[0:rows, 0:P], src_ap_2d, writes=[c.bias_stage])
    ps = c.psum_next()
    p.emit("te", lambda h: h.transpose(ps.h[:, 0:rows], c.bias_stage.h[0:rows, 0:P],
                                       c.ident_f.h[0:rows, 0:rows]),
           reads=[c.bias_stage, c.ident_f], writes=[ps])
    p.emit("ve", lambda h: h.tensor_copy(dst_ap, ps.h[:, 0:rows]),
           reads=[ps], writes=[dst_buf])


def mm_group(p, psum_buf, psum_ap, pairs, read_bufs):
    n = len(pairs)

    def fn(h):
        last = None
        for i, (l, r) in enumerate(pairs):
            last = h.matmul(psum_ap, l, r, start=(i == 0), stop=(i == n - 1))
        return last

    return p.emit("te", fn, reads=list(read_bufs), writes=[psum_buf])



def build_l0ffn(T=T_LOC):
    nc = bass.Bass("TRN2", target_bir_lowering=False)
    x_in = nc.dram_tensor("x", [T, D], F32, kind="ExternalInput").ap()
    w1_in = nc.dram_tensor("W1", [4 * D, D], F32, kind="ExternalInput").ap()
    b1_in = nc.dram_tensor("b1", [4 * D], F32, kind="ExternalInput").ap()
    w2_in = nc.dram_tensor("W2", [D, 4 * D], F32, kind="ExternalInput").ap()
    b2_in = nc.dram_tensor("b2", [D], F32, kind="ExternalInput").ap()
    x_out = nc.dram_tensor("xo", [T, D], F32, kind="ExternalOutput").ap()

    F = 4 * D
    NFT = F // P
    NDT = D // P
    CH = 256
    NCHUNK = T // CH

    with ExitStack() as stack:
        p = Prog(nc, stack)
        c = Common(p)

        w1qT = p.sbuf("w1qT", [P, NDT * F], BF16)
        w2qT = p.sbuf("w2qT", [P, NFT * D], BF16)
        b1col = p.sbuf("b1col", [P, NFT], F32)
        b2row = p.sbuf("b2row", [1, D], BF16)
        p.dma("pool", b1col.h[:, :], b1_in.rearrange("(a p) -> p a", p=P),
              writes=[b1col], allow_slow_non_contiguous=True)
        p.dma("pool", b2row.h[:, :], b2_in.unsqueeze(0), writes=[b2row])

        panels = []
        for i in range(F // P):
            panels.append((w1_in[i * P:(i + 1) * P, :], 1, i, 0, D))
        for i in range(D // P):
            for jh in range(2):
                panels.append((w2_in[i * P:(i + 1) * P, jh * 2048:(jh + 1) * 2048], 2, i, jh, 2048))

        wstr = [p.sbuf(f"wstr{i}", [P, 2048], F32) for i in range(2)]
        nacc = len(panels)
        accg = p.sbuf("accg", [P, nacc], F32)

        for k, (ap, which, i, jh, ncols) in enumerate(panels):
            wb = wstr[k % 2]
            p.dma("sp", wb.h[:, 0:ncols], ap, writes=[wb])
            p.emit("ve", lambda h, wb=wb, k=k, ncols=ncols: h.tensor_reduce(
                accg.h[:, k:k + 1], wb.h[:, 0:ncols], axis=AX.X, op=OP.add,
                apply_absolute_value=True), reads=[wb], writes=[accg])
        gsum = p.sbuf("gsum", [P, 2], F32)
        p.emit("ve", lambda h: h.tensor_reduce(gsum.h[:, 0:1], accg.h[:, 0:NFT], axis=AX.X, op=OP.add),
               reads=[accg], writes=[gsum])
        p.emit("ve", lambda h: h.tensor_reduce(gsum.h[:, 1:2], accg.h[:, NFT:nacc], axis=AX.X, op=OP.add),
               reads=[accg], writes=[gsum])
        psg = c.psum_next()
        p.emit("te", lambda h: h.matmul(psg.h[0:1, 0:2], c.ones_col_f.h[:, 0:1],
                                        gsum.h[:, 0:2], start=True, stop=True),
               reads=[c.ones_col_f, gsum], writes=[psg])
        g2 = p.sbuf("g2", [1, 2], F32)
        p.emit("ve", lambda h: h.tensor_scalar(g2.h[:, :], psg.h[0:1, 0:2], 1.0 / (F * D), None,
                                               OP.mult), reads=[psg], writes=[g2])
        p.emit("ve", lambda h: h.tensor_scalar(g2.h[:, :], g2.h[:, :], 1e-5, None, OP.add),
               reads=[g2], writes=[g2])
        p.emit("ve", lambda h: h.reciprocal(g2.h[:, :], g2.h[:, :]), reads=[g2], writes=[g2])
        psb = c.psum_next()
        p.emit("te", lambda h: h.matmul(psb.h[:, 0:2], c.ones_row_f.h[0:1, :], g2.h[0:1, :],
                                        start=True, stop=True), reads=[c.ones_row_f, g2], writes=[psb])
        rs = p.sbuf("rs", [P, 2], F32)
        p.emit("ve", lambda h: h.tensor_copy(rs.h[:, :], psb.h[:, 0:2]), reads=[psb], writes=[rs])

        qi32 = p.sbuf("qi32", [P, 2048], I32)
        qbf = p.sbuf("qbf", [P, 2048], BF16)
        for k, (ap, which, i, jh, ncols) in enumerate(panels):
            wb = wstr[k % 2]
            p.dma("sp", wb.h[:, 0:ncols], ap, writes=[wb])
            rcol = rs.h[:, (which - 1):which]
            p.emit("ve", lambda h, wb=wb, rcol=rcol, n=ncols: h.tensor_scalar(
                wb.h[:, 0:n], wb.h[:, 0:n], rcol, 1.4995, OP.mult, OP.min),
                reads=[wb, rs], writes=[wb])
            p.emit("ve", lambda h, wb=wb, n=ncols: h.tensor_scalar(
                wb.h[:, 0:n], wb.h[:, 0:n], -1.4995, None, OP.max), reads=[wb], writes=[wb])
            p.emit("ve", lambda h, wb=wb, n=ncols: h.tensor_copy(qi32.h[:, 0:n], wb.h[:, 0:n]),
                   reads=[wb], writes=[qi32])
            p.emit("ve", lambda h, n=ncols: h.tensor_copy(qbf.h[:, 0:n], qi32.h[:, 0:n]),
                   reads=[qi32], writes=[qbf])
            nblk = ncols // P
            for b0 in range(0, nblk, 4):
                srcs, dsts = [], []
                for j in range(b0, min(b0 + 4, nblk)):
                    srcs.append(qbf.h[:, j * P:(j + 1) * P])
                    if which == 1:
                        dsts.append(w1qT.h[:, j * F + i * P: j * F + (i + 1) * P])
                    else:
                        ft = jh * 16 + j
                        dsts.append(w2qT.h[:, ft * D + i * P: ft * D + (i + 1) * P])
                emit_transpose4(p, c, srcs, dsts, [qbf], [w1qT if which == 1 else w2qT])

        xti = [p.sbuf(f"xti{i}", [P, D], F32) for i in range(2)]
        xn = p.sbuf("xn", [P, D], BF16)
        xnT = p.sbuf("xnT", [P, NDT * CH], BF16)
        hT = p.sbuf("hT", [P, NFT * CH], BF16)
        xo = p.sbuf("xo_sb", [P, D], F32)
        TT = CH // P
        for ch in range(NCHUNK):
            xbs = []
            for tt in range(TT):
                t0 = ch * CH + tt * P
                xb = xti[tt % 2]
                xbs.append(xb)
                p.dma("sp", xb.h[:, :], x_in[t0:t0 + P, :], writes=[xb])
                emit_ln(p, c, xb, xb.h[:, :], xn, xn.h[:, :])
                for b0 in range(0, NDT, 4):
                    srcs = [xn.h[:, j * P:(j + 1) * P] for j in range(b0, b0 + 4)]
                    dsts = [xnT.h[:, j * CH + tt * P: j * CH + (tt + 1) * P] for j in range(b0, b0 + 4)]
                    emit_transpose4(p, c, srcs, dsts, [xn], [xnT])
            for ft in range(NFT):
                ps = c.psum_next()
                pairs = [(w1qT.h[:, dt * F + ft * P: dt * F + (ft + 1) * P],
                          xnT.h[:, dt * CH:(dt + 1) * CH]) for dt in range(NDT)]
                mm_group(p, ps, ps.h[:, 0:CH], pairs, [w1qT, xnT])
                p.emit("act", lambda h, ps=ps, ft=ft: h.activation(
                    hT.h[:, ft * CH:(ft + 1) * CH], ps.h[:, 0:CH], AF.Relu,
                    bias=b1col.h[:, ft:ft + 1]), reads=[ps, b1col], writes=[hT])
            for tt in range(TT):
                for dc in range(2):
                    ps = c.psum_next()
                    pairs = [(hT.h[:, ft * CH + tt * P: ft * CH + (tt + 1) * P],
                              w2qT.h[:, ft * D + dc * 512: ft * D + (dc + 1) * 512])
                             for ft in range(NFT)]
                    pairs.append((c.ones_row.h[0:1, 0:P],
                                  b2row.h[0:1, dc * 512:(dc + 1) * 512]))
                    mm_group(p, ps, ps.h[:, 0:512], pairs, [hT, w2qT, c.ones_row, b2row])
                    xb = xbs[tt]
                    p.emit("ve", lambda h, ps=ps, dc=dc, xb=xb: h.scalar_tensor_tensor(
                        xo.h[:, dc * 512:(dc + 1) * 512], ps.h[:, 0:512], 0.5,
                        xb.h[:, dc * 512:(dc + 1) * 512], OP.mult, OP.add),
                        reads=[ps, xb], writes=[xo])
                t0 = ch * CH + tt * P
                p.dma("sp", x_out[t0:t0 + P, :], xo.h[:, :], reads=[xo])
        p.barrier()
        p.finalize()
    return nc



def build_l1attn():
    nc = bass.Bass("TRN2", target_bir_lowering=False)
    xfull = nc.dram_tensor("xfull", [S, D], F32, kind="ExternalInput").ap()
    xq_in = nc.dram_tensor("xq", [T_LOC, D], F32, kind="ExternalInput").ap()
    wq_in = nc.dram_tensor("Wq", [D, D], F32, kind="ExternalInput").ap()
    bq_in = nc.dram_tensor("bq", [D], F32, kind="ExternalInput").ap()
    wk_in = nc.dram_tensor("Wk", [D, D], F32, kind="ExternalInput").ap()
    bk_in = nc.dram_tensor("bk", [D], F32, kind="ExternalInput").ap()
    wv_in = nc.dram_tensor("Wv", [D, D], F32, kind="ExternalInput").ap()
    bv_in = nc.dram_tensor("bv", [D], F32, kind="ExternalInput").ap()
    x_out = nc.dram_tensor("xo", [T_LOC, D], F32, kind="ExternalOutput").ap()

    qT_dram = nc.dram_tensor("qT_scr", [(T_LOC // P) * 8 * P, P], BF16).ap()
    v_dram = nc.dram_tensor("v_scr", [S, D], BF16).ap()

    NT = S // P
    NQT = T_LOC // P
    NDT = D // P

    with ExitStack() as stack:
        p = Prog(nc, stack)
        c = Common(p, n_psum_f32=1)
        pacc = [p.psum(f"pacc{i}", [P, 512], F32) for i in range(6)]

        bkcol = p.sbuf("bkcol", [P, NDT], F32)
        bqcol = p.sbuf("bqcol", [P, NDT], F32)
        bvrow = p.sbuf("bvrow", [1, D], BF16)
        p.dma("pool", bkcol.h[:, :], bk_in.rearrange("(a p) -> p a", p=P),
              writes=[bkcol], allow_slow_non_contiguous=True)
        p.dma("pool", bqcol.h[:, :], bq_in.rearrange("(a p) -> p a", p=P),
              writes=[bqcol], allow_slow_non_contiguous=True)
        p.dma("pool", bvrow.h[:, :], bv_in.unsqueeze(0), writes=[bvrow])

        kT = p.sbuf("kT", [P, NDT * S], BF16)

        p.open_scope()
        xnT = p.sbuf("xnT", [P, NDT * S], BF16)
        xqnT = p.sbuf("xqnT", [P, NDT * T_LOC], BF16)
        xstr = [p.sbuf(f"xstr{i}", [P, D], F32) for i in range(2)]
        xn = p.sbuf("xn_b", [P, D], BF16)
        for dst, src_hbm, ntile, width in ((xnT, xfull, NT, S), (xqnT, xq_in, NQT, T_LOC)):
            for tt in range(ntile):
                xb = xstr[tt % 2]
                p.dma("sp", xb.h[:, :], src_hbm[tt * P:(tt + 1) * P, :], writes=[xb])
                emit_ln(p, c, xb, xb.h[:, :], xn, xn.h[:, :])
                for b0 in range(0, NDT, 4):
                    srcs = [xn.h[:, j * P:(j + 1) * P] for j in range(b0, b0 + 4)]
                    dsts = [dst.h[:, j * width + tt * P: j * width + (tt + 1) * P]
                            for j in range(b0, b0 + 4)]
                    emit_transpose4(p, c, srcs, dsts, [xn], [dst])

        wstr = [p.sbuf(f"l1wstr{i}", [P, D], F32) for i in range(2)]
        wbf = p.sbuf("l1wbf", [P, D], BF16)
        wTst = p.sbuf("l1wTst", [P, NDT * P], BF16)
        qstage = p.sbuf("l1qstage", [P, 512], BF16)

        def stream_wT(w_hbm, i, dst_buf, dst_col_fn):
            wb = wstr[i % 2]
            p.dma("sp", wb.h[:, :], w_hbm[i * P:(i + 1) * P, :], writes=[wb])
            p.emit("ve", lambda h, wb=wb: h.tensor_copy(wbf.h[:, :], wb.h[:, :]),
                   reads=[wb], writes=[wbf])
            for b0 in range(0, NDT, 4):
                srcs = [wbf.h[:, j * P:(j + 1) * P] for j in range(b0, b0 + 4)]
                dsts = [dst_col_fn(j) for j in range(b0, b0 + 4)]
                emit_transpose4(p, c, srcs, dsts, [wbf], [dst_buf])

        for i in range(NDT):
            stream_wT(wk_in, i, wTst, lambda j: wTst.h[:, j * P:(j + 1) * P])
            for ch in range(S // 512):
                ps = c.psum_next()
                pairs = [(wTst.h[:, j * P:(j + 1) * P],
                          xnT.h[:, j * S + ch * 512: j * S + (ch + 1) * 512])
                         for j in range(NDT)]
                mm_group(p, ps, ps.h[:, 0:512], pairs, [wTst, xnT])
                p.emit("act", lambda h, ps=ps, i=i, ch=ch: h.activation(
                    kT.h[:, i * S + ch * 512: i * S + (ch + 1) * 512], ps.h[:, 0:512],
                    AF.Identity, bias=bkcol.h[:, i:i + 1]), reads=[ps, bkcol], writes=[kT])
        for i in range(NDT):
            stream_wT(wq_in, i, wTst, lambda j: wTst.h[:, j * P:(j + 1) * P])
            for ch in range(T_LOC // 512):
                ps = c.psum_next()
                pairs = [(wTst.h[:, j * P:(j + 1) * P],
                          xqnT.h[:, j * T_LOC + ch * 512: j * T_LOC + (ch + 1) * 512])
                         for j in range(NDT)]
                mm_group(p, ps, ps.h[:, 0:512], pairs, [wTst, xqnT])
                p.emit("act", lambda h, ps=ps: h.activation(
                    qstage.h[:, 0:512], ps.h[:, 0:512], AF.Identity,
                    bias=bqcol.h[:, i:i + 1]), reads=[ps, bqcol], writes=[qstage])
                dst = qT_dram.rearrange("(blk p) c -> blk p c", p=P)
                for u in range(4):
                    qt = ch * 4 + u
                    p.dma("sp", dst[qt * NDT + i, :, :], qstage.h[:, u * P:(u + 1) * P],
                          reads=[qstage])
        p.close_scope()

        p.open_scope()
        wvT = p.sbuf("wvT", [P, NDT * D], BF16)
        wstr2 = [p.sbuf(f"l1wstr2{i}", [P, D], F32) for i in range(2)]
        wbf2 = p.sbuf("l1wbf2", [P, D], BF16)
        vstage = p.sbuf("l1vstage", [P, D], BF16)
        for i in range(NDT):
            wb = wstr2[i % 2]
            p.dma("sp", wb.h[:, :], wv_in[i * P:(i + 1) * P, :], writes=[wb])
            p.emit("ve", lambda h, wb=wb: h.tensor_copy(wbf2.h[:, :], wb.h[:, :]),
                   reads=[wb], writes=[wbf2])
            for b0 in range(0, NDT, 4):
                srcs = [wbf2.h[:, j * P:(j + 1) * P] for j in range(b0, b0 + 4)]
                dsts = [wvT.h[:, j * D + i * P: j * D + (i + 1) * P] for j in range(b0, b0 + 4)]
                emit_transpose4(p, c, srcs, dsts, [wbf2], [wvT])
        xstr2 = [p.sbuf(f"xstr2{i}", [P, D], F32) for i in range(2)]
        xn2 = p.sbuf("xn2_b", [P, D], BF16)
        xnTt = p.sbuf("xnTt", [P, NDT * P], BF16)
        for tt in range(NT):
            xb = xstr2[tt % 2]
            p.dma("sp", xb.h[:, :], xfull[tt * P:(tt + 1) * P, :], writes=[xb])
            emit_ln(p, c, xb, xb.h[:, :], xn2, xn2.h[:, :])
            for b0 in range(0, NDT, 4):
                srcs = [xn2.h[:, j * P:(j + 1) * P] for j in range(b0, b0 + 4)]
                dsts = [xnTt.h[:, j * P:(j + 1) * P] for j in range(b0, b0 + 4)]
                emit_transpose4(p, c, srcs, dsts, [xn2], [xnTt])
            for dvc in range(2):
                ps = c.psum_next()
                pairs = [(xnTt.h[:, j * P:(j + 1) * P],
                          wvT.h[:, j * D + dvc * 512: j * D + (dvc + 1) * 512])
                         for j in range(NDT)]
                pairs.append((c.ones_row.h[0:1, 0:P], bvrow.h[0:1, dvc * 512:(dvc + 1) * 512]))
                mm_group(p, ps, ps.h[:, 0:512], pairs, [xnTt, wvT, c.ones_row, bvrow])
                p.emit("ve", lambda h, ps=ps, dvc=dvc: h.tensor_copy(
                    vstage.h[:, dvc * 512:(dvc + 1) * 512], ps.h[:, 0:512]),
                    reads=[ps], writes=[vstage])
            p.dma("sp", v_dram[tt * P:(tt + 1) * P, :], vstage.h[:, :], reads=[vstage])
        p.close_scope()

        p.open_scope()
        GMAX = 3
        qTg = p.sbuf("qTg", [P, GMAX * NDT * P], BF16)
        Sb = p.sbuf("Sb", [P, S], F32)
        Pb = p.sbuf("Pb", [P, S], BF16)
        PT = p.sbuf("PT", [P, GMAX * S], BF16)
        mcols = p.sbuf("mcols", [P, 8], F32)
        negm = p.sbuf("negm", [P, 1], F32)
        lsum = p.sbuf("lsum", [P, 8], F32)
        lcol = p.sbuf("lcol", [P, 1], F32)
        rls = p.sbuf("rls", [P, GMAX], F32)
        vchunk = [p.sbuf(f"vch{i}", [P, D], BF16) for i in range(2)]
        xqt = p.sbuf("xqt", [P, D], F32)
        xo_sb = p.sbuf("xo_l1", [P, D], F32)

        qtiles = list(range(NQT))
        groups = [qtiles[i:i + GMAX] for i in range(0, NQT, GMAX)]
        qT_blk = qT_dram.rearrange("(qt i p) c -> qt p i c", i=NDT, p=P)
        for qts in groups:
            for u, qt in enumerate(qts):
                p.dma("sp",
                      qTg.h[:, u * NDT * P:(u + 1) * NDT * P].rearrange(
                          "p (i c) -> p i c", i=NDT),
                      qT_blk[qt], writes=[qTg])
            for u, qt in enumerate(qts):
                for kc in range(8):
                    ps = c.psum_next()
                    pairs = [(qTg.h[:, u * NDT * P + i * P: u * NDT * P + (i + 1) * P],
                              kT.h[:, i * S + kc * 512: i * S + (kc + 1) * 512])
                             for i in range(NDT)]
                    mm_group(p, ps, ps.h[:, 0:512], pairs, [qTg, kT])
                    p.emit("act", lambda h, ps=ps, kc=kc: h.activation(
                        Sb.h[:, kc * 512:(kc + 1) * 512], ps.h[:, 0:512], AF.Copy,
                        scale=1.0 / 32.0), reads=[ps], writes=[Sb])
                    p.emit("ve", lambda h, kc=kc: h.reduce_max(
                        mcols.h[:, kc:kc + 1], Sb.h[:, kc * 512:(kc + 1) * 512], axis=AX.X),
                        reads=[Sb], writes=[mcols])
                p.emit("ve", lambda h: h.reduce_max(negm.h[:, :], mcols.h[:, :], axis=AX.X,
                                                    negate=True), reads=[mcols], writes=[negm])
                for kc in range(8):
                    p.emit("act", lambda h, kc=kc: h.activation(
                        Pb.h[:, kc * 512:(kc + 1) * 512], Sb.h[:, kc * 512:(kc + 1) * 512],
                        AF.Exp, bias=negm.h[:, 0:1], accum_out=lsum.h[:, kc:kc + 1]),
                        reads=[Sb, negm], writes=[Pb, lsum])
                p.emit("ve", lambda h: h.tensor_reduce(lcol.h[:, :], lsum.h[:, :], axis=AX.X,
                                                       op=OP.add), reads=[lsum], writes=[lcol])
                p.emit("ve", lambda h, u=u: h.reciprocal(rls.h[:, u:u + 1], lcol.h[:, 0:1]),
                       reads=[lcol], writes=[rls])
                for b0 in range(0, NT, 4):
                    srcs = [Pb.h[:, j * P:(j + 1) * P] for j in range(b0, b0 + 4)]
                    dsts = [PT.h[:, u * S + j * P: u * S + (j + 1) * P] for j in range(b0, b0 + 4)]
                    emit_transpose4(p, c, srcs, dsts, [Pb], [PT])
            for kc32 in range(NT):
                vb = vchunk[kc32 % 2]
                p.dma("sp", vb.h[:, :], v_dram[kc32 * P:(kc32 + 1) * P, :], writes=[vb])
                for u in range(len(qts)):
                    for dvc in range(2):
                        acc = pacc[u * 2 + dvc]
                        p.emit("te", lambda h, acc=acc, u=u, kc32=kc32, dvc=dvc, vb=vb: h.matmul(
                            acc.h[:, 0:512],
                            PT.h[:, u * S + kc32 * P: u * S + (kc32 + 1) * P],
                            vb.h[:, dvc * 512:(dvc + 1) * 512],
                            start=(kc32 == 0), stop=(kc32 == NT - 1),
                            skip_group_check=True),
                            reads=[PT, vb], writes=[acc])
            for u, qt in enumerate(qts):
                p.dma("sp", xqt.h[:, :], xq_in[qt * P:(qt + 1) * P, :], writes=[xqt])
                for dvc in range(2):
                    acc = pacc[u * 2 + dvc]
                    p.emit("ve", lambda h, acc=acc, u=u, dvc=dvc: h.scalar_tensor_tensor(
                        xo_sb.h[:, dvc * 512:(dvc + 1) * 512], acc.h[:, 0:512],
                        rls.h[:, u:u + 1], xqt.h[:, dvc * 512:(dvc + 1) * 512],
                        OP.mult, OP.add), reads=[acc, rls, xqt], writes=[xo_sb])
                p.dma("sp", x_out[qt * P:(qt + 1) * P, :], xo_sb.h[:, :], reads=[xo_sb])
        p.close_scope()
        p.barrier()
        p.finalize()
    return nc



def build_l3ffn(T=T_LOC):
    E, FF = 14, 2048
    nc = bass.Bass("TRN2", target_bir_lowering=False)
    x_in = nc.dram_tensor("x", [T, D], F32, kind="ExternalInput").ap()
    rw_in = nc.dram_tensor("router_w", [D, E], F32, kind="ExternalInput").ap()
    w1_in = nc.dram_tensor("w1", [E, D, FF], F32, kind="ExternalInput").ap()
    b1_in = nc.dram_tensor("b1", [E, FF], F32, kind="ExternalInput").ap()
    w2_in = nc.dram_tensor("w2", [E, FF, D], F32, kind="ExternalInput").ap()
    b2_in = nc.dram_tensor("b2", [E, D], F32, kind="ExternalInput").ap()
    x_out = nc.dram_tensor("xo", [T, D], F32, kind="ExternalOutput").ap()

    NDT = D // P
    NFT = FF // P
    NTT = T // P
    CH = 512
    NCH = T // CH

    with ExitStack() as stack:
        p = Prog(nc, stack)
        c = Common(p)
        c.need_identf()

        b1cols = p.sbuf("b1cols", [P, E * NFT], F32)
        b1v = b1_in.rearrange("e f -> (e f)").rearrange("(r p) -> r p", p=P)
        emit_cols_from_1d(p, c, b1v[0:112, :], 112, b1cols, b1cols.h[:, 0:112])
        emit_cols_from_1d(p, c, b1v[112:224, :], 112, b1cols, b1cols.h[:, 112:224])
        b2row3 = p.sbuf("b2row3", [1, D], BF16)
        rwsb = p.sbuf("rwsb", [P, NDT * E], BF16)
        p.dma("pool", rwsb.h[:, :].rearrange("p (dt e) -> p dt e", dt=NDT),
              rw_in.rearrange("(dt p) e -> p dt e", p=P), writes=[rwsb])

        xnT = p.sbuf("xnT3", [P, NDT * T], BF16)
        selc = p.sbuf("selc", [P, NTT * E], F32)
        xstr = [p.sbuf(f"x3str{i}", [P, D], F32) for i in range(2)]
        xn = p.sbuf("xn3", [P, D], BF16)
        lg = p.sbuf("lg3", [P, E], F32)
        mrow = p.sbuf("m3", [P, 1], F32)
        negm = p.sbuf("negm3", [P, 1], F32)
        lsum = p.sbuf("lsum3", [P, 1], F32)
        expl = p.sbuf("expl3", [P, E], F32)
        for tt in range(NTT):
            xb = xstr[tt % 2]
            p.dma("sp", xb.h[:, :], x_in[tt * P:(tt + 1) * P, :], writes=[xb])
            emit_ln(p, c, xb, xb.h[:, :], xn, xn.h[:, :])
            for b0 in range(0, NDT, 4):
                srcs = [xn.h[:, j * P:(j + 1) * P] for j in range(b0, b0 + 4)]
                dsts = [xnT.h[:, j * T + tt * P: j * T + (tt + 1) * P] for j in range(b0, b0 + 4)]
                emit_transpose4(p, c, srcs, dsts, [xn], [xnT])
            ps = c.psum_next()
            pairs = [(xnT.h[:, j * T + tt * P: j * T + (tt + 1) * P],
                      rwsb.h[:, j * E:(j + 1) * E]) for j in range(NDT)]
            mm_group(p, ps, ps.h[:, 0:E], pairs, [xnT, rwsb])
            p.emit("ve", lambda h, ps=ps: h.tensor_copy(lg.h[:, :], ps.h[:, 0:E]),
                   reads=[ps], writes=[lg])
            p.emit("ve", lambda h: h.reduce_max(mrow.h[:, :], lg.h[:, :], axis=AX.X),
                   reads=[lg], writes=[mrow])
            p.emit("ve", lambda h: h.reduce_max(negm.h[:, :], lg.h[:, :], axis=AX.X,
                                                negate=True), reads=[lg], writes=[negm])
            p.emit("act", lambda h: h.activation(expl.h[:, :], lg.h[:, :], AF.Exp,
                                                 bias=negm.h[:, 0:1],
                                                 accum_out=lsum.h[:, 0:1]),
                   reads=[lg, negm], writes=[expl, lsum])
            p.emit("ve", lambda h: h.reciprocal(lsum.h[:, :], lsum.h[:, :]),
                   reads=[lsum], writes=[lsum])
            p.emit("ve", lambda h, tt=tt: h.tensor_scalar(
                selc.h[:, tt * E:(tt + 1) * E], lg.h[:, :], mrow.h[:, 0:1],
                lsum.h[:, 0:1], OP.is_equal, OP.mult),
                reads=[lg, mrow, lsum], writes=[selc])

        w1sb = p.sbuf("w1sb", [P, NDT * FF], BF16)
        w2sb = p.sbuf("w2sb", [P, NFT * D], BF16)
        hT = p.sbuf("hT3", [P, NFT * CH], BF16)
        acc = p.sbuf("acc3", [P, NTT * D], F32)
        for e in range(E):
            p.dma("pool", b2row3.h[:, :], b2_in[e:e + 1, :], writes=[b2row3])
            for dt in range(NDT):
                p.dma("pool", w1sb.h[:, dt * FF:(dt + 1) * FF],
                      w1_in[e, dt * P:(dt + 1) * P, :], writes=[w1sb])
            for ft in range(NFT):
                p.dma("pool", w2sb.h[:, ft * D:(ft + 1) * D],
                      w2_in[e, ft * P:(ft + 1) * P, :], writes=[w2sb])
            for chg in range(NCH):
                for ft in range(NFT):
                    ps = c.psum_next()
                    pairs = [(w1sb.h[:, dt * FF + ft * P: dt * FF + (ft + 1) * P],
                              xnT.h[:, dt * T + chg * CH: dt * T + (chg + 1) * CH])
                             for dt in range(NDT)]
                    mm_group(p, ps, ps.h[:, 0:CH], pairs, [w1sb, xnT])
                    p.emit("act", lambda h, ps=ps, ft=ft, e=e: h.activation(
                        hT.h[:, ft * CH:(ft + 1) * CH], ps.h[:, 0:CH], AF.Gelu,
                        bias=b1cols.h[:, e * NFT + ft: e * NFT + ft + 1]),
                        reads=[ps, b1cols], writes=[hT])
                for ti in range(CH // P):
                    tt = chg * (CH // P) + ti
                    for dc in range(2):
                        ps = c.psum_next()
                        pairs = [(hT.h[:, ft * CH + ti * P: ft * CH + (ti + 1) * P],
                                  w2sb.h[:, ft * D + dc * 512: ft * D + (dc + 1) * 512])
                                 for ft in range(NFT)]
                        pairs.append((c.ones_row.h[0:1, 0:P],
                                      b2row3.h[0:1, dc * 512:(dc + 1) * 512]))
                        mm_group(p, ps, ps.h[:, 0:512], pairs,
                                 [hT, w2sb, c.ones_row, b2row3])
                        accsl = acc.h[:, tt * D + dc * 512: tt * D + dc * 512 + 512]
                        scol = selc.h[:, tt * E + e: tt * E + e + 1]
                        if e == 0:
                            p.emit("ve", lambda h, ps=ps, accsl=accsl, scol=scol:
                                   h.tensor_scalar(accsl, ps.h[:, 0:512], scol, None,
                                                   OP.mult),
                                   reads=[ps, selc], writes=[acc])
                        else:
                            p.emit("ve", lambda h, ps=ps, accsl=accsl, scol=scol:
                                   h.scalar_tensor_tensor(accsl, ps.h[:, 0:512], scol,
                                                          accsl, OP.mult, OP.add),
                                   reads=[ps, selc, acc], writes=[acc])
        xo = p.sbuf("xo3", [P, D], F32)
        for tt in range(NTT):
            xb = xstr[tt % 2]
            p.dma("sp", xb.h[:, :], x_in[tt * P:(tt + 1) * P, :], writes=[xb])
            p.emit("ve", lambda h, tt=tt, xb=xb: h.scalar_tensor_tensor(
                xo.h[:, :], acc.h[:, tt * D:(tt + 1) * D], 0.5, xb.h[:, :],
                OP.mult, OP.add), reads=[acc, xb], writes=[xo])
            p.dma("sp", x_out[tt * P:(tt + 1) * P, :], xo.h[:, :], reads=[xo])
        p.barrier()
        p.finalize()
    return nc



def build_l1ffn(T=T_LOC):
    E, FF = 8, 512
    nc = bass.Bass("TRN2", target_bir_lowering=False)
    x_in = nc.dram_tensor("x", [T, D], F32, kind="ExternalInput").ap()
    w1_in = nc.dram_tensor("w1", [E, D, FF], F32, kind="ExternalInput").ap()
    b1_in = nc.dram_tensor("b1", [E, FF], F32, kind="ExternalInput").ap()
    w2_in = nc.dram_tensor("w2", [E, FF, D], F32, kind="ExternalInput").ap()
    b2_in = nc.dram_tensor("b2", [E, D], F32, kind="ExternalInput").ap()
    cen_in = nc.dram_tensor("centroids", [E, D], F32, kind="ExternalInput").ap()
    wr_in = nc.dram_tensor("w_route", [E, D], F32, kind="ExternalInput").ap()
    x_out = nc.dram_tensor("xo", [T, D], F32, kind="ExternalOutput").ap()

    NDT = D // P
    NFT = FF // P
    NTT = T // P
    CH = 512
    NCH = T // CH

    with ExitStack() as stack:
        p = Prog(nc, stack)
        c = Common(p)
        c.need_identf()

        cn_sb = p.sbuf("cn_sb", [E, D], F32)
        wr_sb = p.sbuf("wr_sb", [E, D], F32)
        p.dma("sp", cn_sb.h[:, :], cen_in, writes=[cn_sb])
        p.dma("sp", wr_sb.h[:, :], wr_in, writes=[wr_sb])
        cnrm = p.sbuf("cnrm", [E, 1], F32)
        p.emit("act", lambda h: h.activation(c.sq.h[0:E, 0:D], cn_sb.h[:, :], AF.Square,
                                             accum_out=cnrm.h[:, 0:1]),
               reads=[cn_sb], writes=[c.sq, cnrm])
        p.emit("act", lambda h: h.activation(cnrm.h[:, :], cnrm.h[:, :], AF.Sqrt),
               reads=[cnrm], writes=[cnrm])
        p.emit("ve", lambda h: h.tensor_scalar(cnrm.h[:, :], cnrm.h[:, :], 1e-12, None,
                                               OP.max), reads=[cnrm], writes=[cnrm])
        p.emit("ve", lambda h: h.reciprocal(cnrm.h[:, :], cnrm.h[:, :]),
               reads=[cnrm], writes=[cnrm])
        p.emit("ve", lambda h: h.tensor_scalar(cn_sb.h[:, :], cn_sb.h[:, :],
                                               cnrm.h[:, 0:1], None, OP.mult),
               reads=[cn_sb, cnrm], writes=[cn_sb])
        cnT = p.sbuf("cnT", [P, NDT * E], BF16)
        wrT = p.sbuf("wrT", [P, NDT * E], BF16)
        for src, dst in ((cn_sb, cnT), (wr_sb, wrT)):
            for dt in range(NDT):
                ps = c.psum_next()
                p.emit("te", lambda h, src=src, dt=dt, ps=ps: h.transpose(
                    ps.h[:, 0:E], src.h[0:E, dt * P:(dt + 1) * P], c.ident_f.h[0:E, 0:E]),
                    reads=[src, c.ident_f], writes=[ps])
                p.emit("ve", lambda h, ps=ps, dst=dst, dt=dt: h.tensor_copy(
                    dst.h[:, dt * E:(dt + 1) * E], ps.h[:, 0:E]), reads=[ps], writes=[dst])

        b1cols = p.sbuf("b1cols1", [P, E * NFT], F32)
        b1v = b1_in.rearrange("e f -> (e f)").rearrange("(r p) -> r p", p=P)
        emit_cols_from_1d(p, c, b1v, E * NFT, b1cols, b1cols.h[:, 0:E * NFT])
        b2row = p.sbuf("b2row1", [1, D], BF16)

        xnT = p.sbuf("xnT1", [P, NDT * T], BF16)
        gatec = p.sbuf("gatec", [P, NTT * E], F32)
        xstr = [p.sbuf(f"x1str{i}", [P, D], F32) for i in range(2)]
        xn = p.sbuf("xn1", [P, D], BF16)
        rnorm = p.sbuf("rnorm", [P, 1], F32)
        lg = p.sbuf("lg1", [P, E], F32)
        lgr = p.sbuf("lgr1", [P, E], F32)
        srt = p.sbuf("srt1", [P, 8], F32)
        nm = p.sbuf("nm1", [P, 1], F32)
        msk = p.sbuf("msk1", [P, E], F32)
        gsum = p.sbuf("gsum1", [P, 1], F32)
        for tt in range(NTT):
            xb = xstr[tt % 2]
            p.dma("sp", xb.h[:, :], x_in[tt * P:(tt + 1) * P, :], writes=[xb])
            emit_ln(p, c, xb, xb.h[:, :], xn, xn.h[:, :])
            p.emit("act", lambda h: h.activation(rnorm.h[:, :], c.var.h[:, :], AF.Sqrt,
                                                 scale=1024.0), reads=[c.var], writes=[rnorm])
            p.emit("ve", lambda h: h.tensor_scalar(rnorm.h[:, :], rnorm.h[:, :],
                                                   c.rstd.h[:, 0:1], 1e-12, OP.mult, OP.max),
                   reads=[rnorm, c.rstd], writes=[rnorm])
            p.emit("ve", lambda h: h.reciprocal(rnorm.h[:, :], rnorm.h[:, :]),
                   reads=[rnorm], writes=[rnorm])
            for b0 in range(0, NDT, 4):
                srcs = [xn.h[:, j * P:(j + 1) * P] for j in range(b0, b0 + 4)]
                dsts = [xnT.h[:, j * T + tt * P: j * T + (tt + 1) * P] for j in range(b0, b0 + 4)]
                emit_transpose4(p, c, srcs, dsts, [xn], [xnT])
            psc = c.psum_next()
            pairs = [(xnT.h[:, j * T + tt * P: j * T + (tt + 1) * P],
                      cnT.h[:, j * E:(j + 1) * E]) for j in range(NDT)]
            mm_group(p, psc, psc.h[:, 0:E], pairs, [xnT, cnT])
            psr = c.psum_next()
            pairs = [(xnT.h[:, j * T + tt * P: j * T + (tt + 1) * P],
                      wrT.h[:, j * E:(j + 1) * E]) for j in range(NDT)]
            mm_group(p, psr, psr.h[:, 0:E], pairs, [xnT, wrT])
            p.emit("ve", lambda h, psr=psr: h.tensor_copy(lgr.h[:, :], psr.h[:, 0:E]),
                   reads=[psr], writes=[lgr])
            p.emit("ve", lambda h, psc=psc: h.scalar_tensor_tensor(
                lg.h[:, :], psc.h[:, 0:E], rnorm.h[:, 0:1], lgr.h[:, :], OP.mult, OP.add),
                reads=[psc, rnorm, lgr], writes=[lg])
            p.emit("ve", lambda h: h.max(srt.h[:, :], lg.h[:, :]), reads=[lg], writes=[srt])
            p.emit("ve", lambda h: h.tensor_scalar(nm.h[:, :], srt.h[:, 0:1], -1.0, None,
                                                   OP.mult), reads=[srt], writes=[nm])
            p.emit("ve", lambda h: h.tensor_scalar(msk.h[:, :], lg.h[:, :], srt.h[:, 1:2],
                                                   None, OP.is_ge), reads=[lg, srt], writes=[msk])
            p.emit("act", lambda h: h.activation(lg.h[:, :], lg.h[:, :], AF.Exp,
                                                 bias=nm.h[:, 0:1]),
                   reads=[lg, nm], writes=[lg])
            p.emit("ve", lambda h: h.tensor_tensor(lg.h[:, :], lg.h[:, :], msk.h[:, :],
                                                   OP.mult), reads=[lg, msk], writes=[lg])
            p.emit("ve", lambda h: h.tensor_reduce(gsum.h[:, :], lg.h[:, :], axis=AX.X,
                                                   op=OP.add), reads=[lg], writes=[gsum])
            p.emit("ve", lambda h: h.reciprocal(gsum.h[:, :], gsum.h[:, :]),
                   reads=[gsum], writes=[gsum])
            p.emit("ve", lambda h, tt=tt: h.tensor_scalar(
                gatec.h[:, tt * E:(tt + 1) * E], lg.h[:, :], gsum.h[:, 0:1], None, OP.mult),
                reads=[lg, gsum], writes=[gatec])

        w1sb = p.sbuf("w1sb1", [P, NDT * FF], BF16)
        w2sb = p.sbuf("w2sb1", [P, NFT * D], BF16)
        hT = p.sbuf("hT1", [P, NFT * CH], BF16)
        acc = p.sbuf("acc1f", [P, NTT * D], F32)
        for e in range(E):
            p.dma("pool", b2row.h[:, :], b2_in[e:e + 1, :], writes=[b2row])
            for dt in range(NDT):
                p.dma("pool", w1sb.h[:, dt * FF:(dt + 1) * FF],
                      w1_in[e, dt * P:(dt + 1) * P, :], writes=[w1sb])
            for ft in range(NFT):
                p.dma("pool", w2sb.h[:, ft * D:(ft + 1) * D],
                      w2_in[e, ft * P:(ft + 1) * P, :], writes=[w2sb])
            for chg in range(NCH):
                for ft in range(NFT):
                    ps = c.psum_next()
                    pairs = [(w1sb.h[:, dt * FF + ft * P: dt * FF + (ft + 1) * P],
                              xnT.h[:, dt * T + chg * CH: dt * T + (chg + 1) * CH])
                             for dt in range(NDT)]
                    mm_group(p, ps, ps.h[:, 0:CH], pairs, [w1sb, xnT])
                    p.emit("act", lambda h, ps=ps, ft=ft, e=e: h.activation(
                        hT.h[:, ft * CH:(ft + 1) * CH], ps.h[:, 0:CH], AF.Gelu,
                        bias=b1cols.h[:, e * NFT + ft: e * NFT + ft + 1]),
                        reads=[ps, b1cols], writes=[hT])
                for ti in range(CH // P):
                    tt = chg * (CH // P) + ti
                    for dc in range(2):
                        ps = c.psum_next()
                        pairs = [(hT.h[:, ft * CH + ti * P: ft * CH + (ti + 1) * P],
                                  w2sb.h[:, ft * D + dc * 512: ft * D + (dc + 1) * 512])
                                 for ft in range(NFT)]
                        pairs.append((c.ones_row.h[0:1, 0:P],
                                      b2row.h[0:1, dc * 512:(dc + 1) * 512]))
                        mm_group(p, ps, ps.h[:, 0:512], pairs,
                                 [hT, w2sb, c.ones_row, b2row])
                        accsl = acc.h[:, tt * D + dc * 512: tt * D + dc * 512 + 512]
                        scol = gatec.h[:, tt * E + e: tt * E + e + 1]
                        if e == 0:
                            p.emit("ve", lambda h, ps=ps, accsl=accsl, scol=scol:
                                   h.tensor_scalar(accsl, ps.h[:, 0:512], scol, None,
                                                   OP.mult), reads=[ps, gatec], writes=[acc])
                        else:
                            p.emit("ve", lambda h, ps=ps, accsl=accsl, scol=scol:
                                   h.scalar_tensor_tensor(accsl, ps.h[:, 0:512], scol,
                                                          accsl, OP.mult, OP.add),
                                   reads=[ps, gatec, acc], writes=[acc])
        xo = p.sbuf("xo1", [P, D], F32)
        for tt in range(NTT):
            xb = xstr[tt % 2]
            p.dma("sp", xb.h[:, :], x_in[tt * P:(tt + 1) * P, :], writes=[xb])
            p.emit("ve", lambda h, tt=tt, xb=xb: h.scalar_tensor_tensor(
                xo.h[:, :], acc.h[:, tt * D:(tt + 1) * D], 0.5, xb.h[:, :],
                OP.mult, OP.add), reads=[acc, xb], writes=[xo])
            p.dma("sp", x_out[tt * P:(tt + 1) * P, :], xo.h[:, :], reads=[xo])
        p.barrier()
        p.finalize()
    return nc



def build_l2ffn(T=T_LOC):
    E, I, SI = 8, 512, 2048
    RSC = 2.5
    nc = bass.Bass("TRN2", target_bir_lowering=False)
    x_in = nc.dram_tensor("x", [T, D], F32, kind="ExternalInput").ap()
    gw_in = nc.dram_tensor("gate_weight", [E, D], F32, kind="ExternalInput").ap()
    gb_in = nc.dram_tensor("gate_bias", [E], F32, kind="ExternalInput").ap()
    up_in = nc.dram_tensor("up_weights", [E, I, D], F32, kind="ExternalInput").ap()
    dn_in = nc.dram_tensor("down_weights", [E, D, I], F32, kind="ExternalInput").ap()
    su_in = nc.dram_tensor("shared_up_weight", [SI, D], F32, kind="ExternalInput").ap()
    sd_in = nc.dram_tensor("shared_down_weight", [D, SI], F32, kind="ExternalInput").ap()
    x_out = nc.dram_tensor("xo", [T, D], F32, kind="ExternalOutput").ap()

    NDT = D // P
    NTT = T // P
    CH = 512
    NCH = T // CH

    with ExitStack() as stack:
        p = Prog(nc, stack)
        c = Common(p)
        c.need_identf()

        gw_sb = p.sbuf("gw_sb", [E, D], F32)
        p.dma("sp", gw_sb.h[:, :], gw_in, writes=[gw_sb])
        gwT = p.sbuf("gwT", [P, NDT * E], BF16)
        for dt in range(NDT):
            ps = c.psum_next()
            p.emit("te", lambda h, dt=dt, ps=ps: h.transpose(
                ps.h[:, 0:E], gw_sb.h[0:E, dt * P:(dt + 1) * P], c.ident_f.h[0:E, 0:E]),
                reads=[gw_sb, c.ident_f], writes=[ps])
            p.emit("ve", lambda h, ps=ps, dt=dt: h.tensor_copy(
                gwT.h[:, dt * E:(dt + 1) * E], ps.h[:, 0:E]), reads=[ps], writes=[gwT])
        gbrow = p.sbuf("gbrow", [1, E], F32)
        p.dma("sp", gbrow.h[:, :], gb_in.unsqueeze(0), writes=[gbrow])
        gbbc = p.sbuf("gbbc", [P, E], F32)
        psb = c.psum_next()
        p.emit("te", lambda h: h.matmul(psb.h[:, 0:E], c.ones_row_f.h[0:1, :],
                                        gbrow.h[0:1, :], start=True, stop=True),
               reads=[c.ones_row_f, gbrow], writes=[psb])
        p.emit("ve", lambda h: h.tensor_copy(gbbc.h[:, :], psb.h[:, 0:E]),
               reads=[psb], writes=[gbbc])

        xnT = p.sbuf("xnT2", [P, NDT * T], BF16)
        cwc = p.sbuf("cwc", [P, NTT * E], F32)
        xstr = [p.sbuf(f"x2str{i}", [P, D], F32) for i in range(2)]
        xn = p.sbuf("xn2f", [P, D], BF16)
        sco = p.sbuf("sco", [P, E], F32)
        sfc = p.sbuf("sfc", [P, E], F32)
        gpad = p.sbuf("gpad", [P, E], F32)
        srtg = p.sbuf("srtg", [P, 8], F32)
        gs = p.sbuf("gs", [P, 2], F32)
        cmp = p.sbuf("cmp2", [P, 1], F32)
        msel = p.sbuf("msel", [P, E], F32)
        mskd = p.sbuf("mskd", [P, E], F32)
        srt2 = p.sbuf("srt2", [P, 8], F32)
        m2 = p.sbuf("m2", [P, E], F32)
        wsum = p.sbuf("wsum", [P, 1], F32)
        p.emit("pool", lambda h: h.memset(gpad.h[:, :], -1e30), writes=[gpad])
        for tt in range(NTT):
            xb = xstr[tt % 2]
            p.dma("sp", xb.h[:, :], x_in[tt * P:(tt + 1) * P, :], writes=[xb])
            emit_ln(p, c, xb, xb.h[:, :], xn, xn.h[:, :])
            for b0 in range(0, NDT, 4):
                srcs = [xn.h[:, j * P:(j + 1) * P] for j in range(b0, b0 + 4)]
                dsts = [xnT.h[:, j * T + tt * P: j * T + (tt + 1) * P] for j in range(b0, b0 + 4)]
                emit_transpose4(p, c, srcs, dsts, [xn], [xnT])
            ps = c.psum_next()
            pairs = [(xnT.h[:, j * T + tt * P: j * T + (tt + 1) * P],
                      gwT.h[:, j * E:(j + 1) * E]) for j in range(NDT)]
            mm_group(p, ps, ps.h[:, 0:E], pairs, [xnT, gwT])
            p.emit("act", lambda h, ps=ps: h.activation(sco.h[:, :], ps.h[:, 0:E],
                                                        AF.Sigmoid), reads=[ps], writes=[sco])
            p.emit("ve", lambda h: h.tensor_add(sfc.h[:, :], sco.h[:, :], gbbc.h[:, :]),
                   reads=[sco, gbbc], writes=[sfc])
            for g in range(2):
                p.emit("ve", lambda h, g=g: h.tensor_copy(gpad.h[:, 0:4],
                                                          sfc.h[:, g * 4:(g + 1) * 4]),
                       reads=[sfc], writes=[gpad])
                p.emit("ve", lambda h: h.max(srtg.h[:, :], gpad.h[:, :]),
                       reads=[gpad], writes=[srtg])
                p.emit("ve", lambda h, g=g: h.tensor_add(gs.h[:, g:g + 1], srtg.h[:, 0:1],
                                                         srtg.h[:, 1:2]),
                       reads=[srtg], writes=[gs])
            p.emit("ve", lambda h: h.tensor_tensor(cmp.h[:, :], gs.h[:, 1:2], gs.h[:, 0:1],
                                                   OP.is_gt), reads=[gs], writes=[cmp])
            p.emit("ve", lambda h: h.tensor_copy(msel.h[:, 4:8],
                                                 cmp.h[:, 0:1].broadcast_to((P, 4))),
                   reads=[cmp], writes=[msel])
            p.emit("ve", lambda h: h.tensor_scalar(msel.h[:, 0:4],
                                                   cmp.h[:, 0:1].broadcast_to((P, 4)),
                                                   1.0, -1.0, OP.subtract, OP.mult),
                   reads=[cmp], writes=[msel])
            p.emit("ve", lambda h: h.tensor_mul(mskd.h[:, :], sfc.h[:, :], msel.h[:, :]),
                   reads=[sfc, msel], writes=[mskd])
            p.emit("ve", lambda h: h.max(srt2.h[:, :], mskd.h[:, :]),
                   reads=[mskd], writes=[srt2])
            p.emit("ve", lambda h: h.tensor_scalar(m2.h[:, :], mskd.h[:, :], srt2.h[:, 1:2],
                                                   None, OP.is_ge), reads=[mskd, srt2],
                   writes=[m2])
            p.emit("ve", lambda h: h.tensor_scalar(gpad.h[:, :], mskd.h[:, :], 0.0, None,
                                                   OP.is_gt), reads=[mskd], writes=[gpad])
            p.emit("ve", lambda h: h.tensor_mul(m2.h[:, :], m2.h[:, :], gpad.h[:, :]),
                   reads=[m2, gpad], writes=[m2])
            p.emit("pool", lambda h: h.memset(gpad.h[:, :], -1e30), writes=[gpad])
            p.emit("ve", lambda h: h.tensor_mul(m2.h[:, :], m2.h[:, :], sco.h[:, :]),
                   reads=[m2, sco], writes=[m2])
            p.emit("ve", lambda h: h.tensor_reduce(wsum.h[:, :], m2.h[:, :], axis=AX.X,
                                                   op=OP.add), reads=[m2], writes=[wsum])
            p.emit("ve", lambda h: h.tensor_scalar(wsum.h[:, :], wsum.h[:, :], 1e-20, None,
                                                   OP.add), reads=[wsum], writes=[wsum])
            p.emit("ve", lambda h: h.reciprocal(wsum.h[:, :], wsum.h[:, :]),
                   reads=[wsum], writes=[wsum])
            p.emit("ve", lambda h, tt=tt: h.tensor_scalar(
                cwc.h[:, tt * E:(tt + 1) * E], m2.h[:, :], wsum.h[:, 0:1], RSC,
                OP.mult, OP.mult), reads=[m2, wsum], writes=[cwc])

        acc = p.sbuf("acc2f", [P, NTT * D], F32)

        p.open_scope()
        suT = p.sbuf("suT", [P, NDT * SI], BF16)
        sdT = p.sbuf("sdT", [P, (SI // P) * D], BF16)
        wbf2k = p.sbuf("w2bf2k", [P, SI], BF16)
        for i in range(SI // P):
            p.dma("pool", wbf2k.h[:, 0:D], su_in[i * P:(i + 1) * P, :], writes=[wbf2k])
            for b0 in range(0, NDT, 4):
                srcs = [wbf2k.h[:, j * P:(j + 1) * P] for j in range(b0, b0 + 4)]
                dsts = [suT.h[:, j * SI + i * P: j * SI + (i + 1) * P] for j in range(b0, b0 + 4)]
                emit_transpose4(p, c, srcs, dsts, [wbf2k], [suT])
        for i in range(NDT):
            p.dma("pool", wbf2k.h[:, :], sd_in[i * P:(i + 1) * P, :], writes=[wbf2k])
            for b0 in range(0, SI // P, 4):
                srcs = [wbf2k.h[:, j * P:(j + 1) * P] for j in range(b0, b0 + 4)]
                dsts = [sdT.h[:, j * D + i * P: j * D + (i + 1) * P] for j in range(b0, b0 + 4)]
                emit_transpose4(p, c, srcs, dsts, [wbf2k], [sdT])
        shT = p.sbuf("shT", [P, (SI // P) * CH], BF16)
        for chg in range(NCH):
            for st_ in range(SI // P):
                ps = c.psum_next()
                pairs = [(suT.h[:, j * SI + st_ * P: j * SI + (st_ + 1) * P],
                          xnT.h[:, j * T + chg * CH: j * T + (chg + 1) * CH])
                         for j in range(NDT)]
                mm_group(p, ps, ps.h[:, 0:CH], pairs, [suT, xnT])
                p.emit("act", lambda h, ps=ps, st_=st_: h.activation(
                    shT.h[:, st_ * CH:(st_ + 1) * CH], ps.h[:, 0:CH], AF.Relu),
                    reads=[ps], writes=[shT])
            p.emit("ve", lambda h: h.tensor_mul(shT.h[:, :], shT.h[:, :], shT.h[:, :]),
                   reads=[shT], writes=[shT])
            for ti in range(CH // P):
                tt = chg * (CH // P) + ti
                for dc in range(2):
                    ps = c.psum_next()
                    pairs = [(shT.h[:, j * CH + ti * P: j * CH + (ti + 1) * P],
                              sdT.h[:, j * D + dc * 512: j * D + (dc + 1) * 512])
                             for j in range(SI // P)]
                    mm_group(p, ps, ps.h[:, 0:512], pairs, [shT, sdT])
                    p.emit("ve", lambda h, ps=ps, tt=tt, dc=dc: h.tensor_copy(
                        acc.h[:, tt * D + dc * 512: tt * D + dc * 512 + 512], ps.h[:, 0:512]),
                        reads=[ps], writes=[acc])
        p.close_scope()

        p.open_scope()
        upT = p.sbuf("upT", [P, NDT * I], BF16)
        dnT = p.sbuf("dnT", [P, (I // P) * D], BF16)
        wbf3 = p.sbuf("w2bf3", [P, D], BF16)
        hTe = p.sbuf("hTe", [P, (I // P) * CH], BF16)
        for e in range(E):
            for i in range(I // P):
                p.dma("pool", wbf3.h[:, :], up_in[e, i * P:(i + 1) * P, :], writes=[wbf3])
                for b0 in range(0, NDT, 4):
                    srcs = [wbf3.h[:, j * P:(j + 1) * P] for j in range(b0, b0 + 4)]
                    dsts = [upT.h[:, j * I + i * P: j * I + (i + 1) * P] for j in range(b0, b0 + 4)]
                    emit_transpose4(p, c, srcs, dsts, [wbf3], [upT])
            for i in range(NDT):
                p.dma("pool", wbf3.h[:, 0:I], dn_in[e, i * P:(i + 1) * P, :], writes=[wbf3])
                for b0 in range(0, I // P, 4):
                    srcs = [wbf3.h[:, j * P:(j + 1) * P] for j in range(b0, b0 + 4)]
                    dsts = [dnT.h[:, j * D + i * P: j * D + (i + 1) * P] for j in range(b0, b0 + 4)]
                    emit_transpose4(p, c, srcs, dsts, [wbf3], [dnT])
            for chg in range(NCH):
                for it in range(I // P):
                    ps = c.psum_next()
                    pairs = [(upT.h[:, j * I + it * P: j * I + (it + 1) * P],
                              xnT.h[:, j * T + chg * CH: j * T + (chg + 1) * CH])
                             for j in range(NDT)]
                    mm_group(p, ps, ps.h[:, 0:CH], pairs, [upT, xnT])
                    p.emit("act", lambda h, ps=ps, it=it: h.activation(
                        hTe.h[:, it * CH:(it + 1) * CH], ps.h[:, 0:CH], AF.Relu),
                        reads=[ps], writes=[hTe])
                p.emit("ve", lambda h: h.tensor_mul(hTe.h[:, :], hTe.h[:, :], hTe.h[:, :]),
                       reads=[hTe], writes=[hTe])
                for ti in range(CH // P):
                    tt = chg * (CH // P) + ti
                    for dc in range(2):
                        ps = c.psum_next()
                        pairs = [(hTe.h[:, j * CH + ti * P: j * CH + (ti + 1) * P],
                                  dnT.h[:, j * D + dc * 512: j * D + (dc + 1) * 512])
                                 for j in range(I // P)]
                        mm_group(p, ps, ps.h[:, 0:512], pairs, [hTe, dnT])
                        accsl = acc.h[:, tt * D + dc * 512: tt * D + dc * 512 + 512]
                        scol = cwc.h[:, tt * E + e: tt * E + e + 1]
                        p.emit("ve", lambda h, ps=ps, accsl=accsl, scol=scol:
                               h.scalar_tensor_tensor(accsl, ps.h[:, 0:512], scol,
                                                      accsl, OP.mult, OP.add),
                               reads=[ps, cwc, acc], writes=[acc])
        p.close_scope()

        xo = p.sbuf("xo2", [P, D], F32)
        for tt in range(NTT):
            xb = xstr[tt % 2]
            p.dma("sp", xb.h[:, :], x_in[tt * P:(tt + 1) * P, :], writes=[xb])
            p.emit("ve", lambda h, tt=tt, xb=xb: h.scalar_tensor_tensor(
                xo.h[:, :], acc.h[:, tt * D:(tt + 1) * D], 0.5, xb.h[:, :],
                OP.mult, OP.add), reads=[acc, xb], writes=[xo])
            p.dma("sp", x_out[tt * P:(tt + 1) * P, :], xo.h[:, :], reads=[xo])
        p.barrier()
        p.finalize()
    return nc



def build_l2attn():
    H, DH = 16, 64
    SHIFT = -20.0
    nc = bass.Bass("TRN2", target_bir_lowering=False)
    xfull = nc.dram_tensor("xfull", [S, D], F32, kind="ExternalInput").ap()
    xq_in = nc.dram_tensor("xq", [T_LOC, D], F32, kind="ExternalInput").ap()
    wq_in = nc.dram_tensor("Wq", [D, D], F32, kind="ExternalInput").ap()
    wk_in = nc.dram_tensor("Wk", [D, D], F32, kind="ExternalInput").ap()
    wv_in = nc.dram_tensor("Wv", [D, D], F32, kind="ExternalInput").ap()
    wp_in = nc.dram_tensor("Wproj", [D, D], F32, kind="ExternalInput").ap()
    bp_in = nc.dram_tensor("bproj", [D], F32, kind="ExternalInput").ap()
    wpre_in = nc.dram_tensor("Wpre", [D, D], F32, kind="ExternalInput").ap()
    bpre_in = nc.dram_tensor("bpre", [D], F32, kind="ExternalInput").ap()
    pi_in = nc.dram_tensor("pi", [1], F32, kind="ExternalInput").ap()
    x_out = nc.dram_tensor("xo", [T_LOC, D], F32, kind="ExternalOutput").ap()

    NDT = D // P
    NT = S // P
    NQT = T_LOC // P
    G = 4
    GQ = G * P

    kT_scr = nc.dram_tensor("kT_scr", [NDT * P, S], BF16).ap()
    pre_scr = nc.dram_tensor("pre_scr", [NT * NDT * P, P], BF16).ap()
    qT_scr = nc.dram_tensor("qT_scr2", [NQT * NDT * P, P], BF16).ap()
    pq_scr = nc.dram_tensor("pq_scr", [NQT * NDT * P, P], BF16).ap()
    v_scr = nc.dram_tensor("v_scr2", [S, D], BF16).ap()

    with ExitStack() as stack:
        p = Prog(nc, stack)
        c = Common(p, n_psum_f32=4)
        c.need_identf()

        pisb = p.sbuf("pisb", [1, 1], F32)
        p.dma("sp", pisb.h[:, :], pi_in.unsqueeze(0), writes=[pisb])
        picol = p.sbuf("picol", [P, 1], F32)
        psx = c.psum_next()
        p.emit("te", lambda h: h.matmul(psx.h[:, 0:1], c.ones_row_f.h[0:1, :],
                                        pisb.h[0:1, :], start=True, stop=True),
               reads=[c.ones_row_f, pisb], writes=[psx])
        p.emit("ve", lambda h: h.tensor_scalar(picol.h[:, :], psx.h[:, 0:1], 1.0 / 32.0,
                                               None, OP.mult), reads=[psx], writes=[picol])
        bprecol = p.sbuf("bprecol", [P, NDT], F32)
        emit_cols_from_1d(p, c, bpre_in.rearrange("(r p) -> r p", p=P), NDT,
                          bprecol, bprecol.h[:, 0:NDT])
        bprow = p.sbuf("bprow", [1, D], BF16)
        p.dma("pool", bprow.h[:, :], bp_in.unsqueeze(0), writes=[bprow])

        p.open_scope()
        xnT = p.sbuf("xnT2a", [P, NDT * S], BF16)
        xqnT = p.sbuf("xqnT2a", [P, NDT * T_LOC], BF16)
        xstr = [p.sbuf(f"x2astr{i}", [P, D], F32) for i in range(2)]
        xn = p.sbuf("xn2a", [P, D], BF16)
        for dst, src_hbm, ntile, width in ((xnT, xfull, NT, S), (xqnT, xq_in, NQT, T_LOC)):
            for tt in range(ntile):
                xb = xstr[tt % 2]
                p.dma("sp", xb.h[:, :], src_hbm[tt * P:(tt + 1) * P, :], writes=[xb])
                emit_ln(p, c, xb, xb.h[:, :], xn, xn.h[:, :])
                for b0 in range(0, NDT, 4):
                    srcs = [xn.h[:, j * P:(j + 1) * P] for j in range(b0, b0 + 4)]
                    dsts = [dst.h[:, j * width + tt * P: j * width + (tt + 1) * P]
                            for j in range(b0, b0 + 4)]
                    emit_transpose4(p, c, srcs, dsts, [xn], [dst])

        wsb = p.sbuf("w2asb", [P, NDT * D], BF16)
        stg = p.sbuf("stg2a", [P, 512], BF16)

        def load_w(w_hbm):
            for dt in range(NDT):
                p.dma("pool", wsb.h[:, dt * D:(dt + 1) * D],
                      w_hbm[dt * P:(dt + 1) * P, :], writes=[wsb])

        qT_blk = qT_scr.rearrange("(qt i pp) cc -> qt i pp cc", i=NDT, pp=P)
        pq_blk = pq_scr.rearrange("(qt i pp) cc -> qt i pp cc", i=NDT, pp=P)
        pre_blk = pre_scr.rearrange("(kt i pp) cc -> kt i pp cc", i=NDT, pp=P)

        load_w(wk_in)
        for i in range(NDT):
            for ch in range(S // 512):
                ps = c.psum_next()
                pairs = [(wsb.h[:, dt * D + i * P: dt * D + (i + 1) * P],
                          xnT.h[:, dt * S + ch * 512: dt * S + (ch + 1) * 512])
                         for dt in range(NDT)]
                mm_group(p, ps, ps.h[:, 0:512], pairs, [wsb, xnT])
                p.emit("act", lambda h, ps=ps: h.activation(
                    stg.h[:, 0:512], ps.h[:, 0:512], AF.Copy, scale=0.125),
                    reads=[ps], writes=[stg])
                p.dma("sp", kT_scr[i * P:(i + 1) * P, ch * 512:(ch + 1) * 512],
                      stg.h[:, 0:512], reads=[stg])
        load_w(wq_in)
        for i in range(NDT):
            for ch in range(T_LOC // 512):
                ps = c.psum_next()
                pairs = [(wsb.h[:, dt * D + i * P: dt * D + (i + 1) * P],
                          xqnT.h[:, dt * T_LOC + ch * 512: dt * T_LOC + (ch + 1) * 512])
                         for dt in range(NDT)]
                mm_group(p, ps, ps.h[:, 0:512], pairs, [wsb, xqnT])
                p.emit("ve", lambda h, ps=ps: h.tensor_copy(stg.h[:, 0:512], ps.h[:, 0:512]),
                       reads=[ps], writes=[stg])
                for u in range(4):
                    qt = ch * 4 + u
                    p.dma("sp", qT_blk[qt, i], stg.h[:, u * P:(u + 1) * P], reads=[stg])
        load_w(wpre_in)
        for i in range(NDT):
            for ch in range(S // 512):
                ps = c.psum_next()
                pairs = [(wsb.h[:, dt * D + i * P: dt * D + (i + 1) * P],
                          xnT.h[:, dt * S + ch * 512: dt * S + (ch + 1) * 512])
                         for dt in range(NDT)]
                mm_group(p, ps, ps.h[:, 0:512], pairs, [wsb, xnT])
                p.emit("act", lambda h, ps=ps, i=i: h.activation(
                    ps.h[:, 0:512], ps.h[:, 0:512], AF.Silu, bias=bprecol.h[:, i:i + 1]),
                    reads=[ps, bprecol], writes=[ps])
                p.emit("ve", lambda h, ps=ps: h.tensor_scalar(
                    stg.h[:, 0:512], ps.h[:, 0:512], picol.h[:, 0:1], None, OP.mult),
                    reads=[ps, picol], writes=[stg])
                for u in range(4):
                    kt = ch * 4 + u
                    p.dma("sp", pre_blk[kt, i], stg.h[:, u * P:(u + 1) * P], reads=[stg])
        for i in range(NDT):
            for ch in range(T_LOC // 512):
                ps = c.psum_next()
                pairs = [(wsb.h[:, dt * D + i * P: dt * D + (i + 1) * P],
                          xqnT.h[:, dt * T_LOC + ch * 512: dt * T_LOC + (ch + 1) * 512])
                         for dt in range(NDT)]
                mm_group(p, ps, ps.h[:, 0:512], pairs, [wsb, xqnT])
                p.emit("act", lambda h, ps=ps, i=i: h.activation(
                    stg.h[:, 0:512], ps.h[:, 0:512], AF.Silu, bias=bprecol.h[:, i:i + 1]),
                    reads=[ps, bprecol], writes=[stg])
                for u in range(4):
                    qt = ch * 4 + u
                    p.dma("sp", pq_blk[qt, i], stg.h[:, u * P:(u + 1) * P], reads=[stg])
        load_w(wv_in)
        vstg = p.sbuf("vstg2a", [P, D], BF16)
        for tt in range(NT):
            for dvc in range(2):
                ps = c.psum_next()
                pairs = [(xnT.h[:, dt * S + tt * P: dt * S + (tt + 1) * P],
                          wsb.h[:, dt * D + dvc * 512:dt * D + (dvc + 1) * 512])
                         for dt in range(NDT)]
                mm_group(p, ps, ps.h[:, 0:512], pairs, [xnT, wsb])
                p.emit("ve", lambda h, ps=ps, dvc=dvc: h.tensor_copy(
                    vstg.h[:, dvc * 512:(dvc + 1) * 512], ps.h[:, 0:512]),
                    reads=[ps], writes=[vstg])
            p.dma("sp", v_scr[tt * P:(tt + 1) * P, :], vstg.h[:, :], reads=[vstg])
        p.close_scope()

        p.open_scope()
        qTg = p.sbuf("qTg2", [P, NDT * GQ], BF16)
        pqg = p.sbuf("pqg2", [P, NDT * GQ], BF16)
        SLg = p.sbuf("SLg", [P, NT * GQ], BF16)
        Pg = p.sbuf("Pg2", [P, NT * GQ], BF16)
        preb = [p.sbuf(f"preb{i}", [P, NDT * P], BF16) for i in range(2)]
        kTi = [p.sbuf(f"kTi{i}", [P, S], BF16) for i in range(2)]
        vb65 = [p.sbuf(f"vb65_{i}", [P, NT * (DH + 1)], BF16) for i in range(2)]
        lrow = p.sbuf("lrow", [1, GQ], F32)
        rlbc = p.sbuf("rlbc", [P, GQ], F32)
        outTg = p.sbuf("outTg", [P, NDT * GQ], BF16)
        wpsb = p.sbuf("wpsb", [P, NDT * D], BF16)
        xqt2 = p.sbuf("xqt2", [P, D], F32)
        xnf = p.sbuf("xnf2", [P, D], F32)
        xo2 = p.sbuf("xo2a", [P, D], F32)
        tmp2 = p.sbuf("tmp2a", [P, D], F32)
        for vbb in vb65:
            p.emit("pool", lambda h, vbb=vbb: h.memset(
                bass.AP(vbb.h.tensor, DH * 2, [[(DH + 1) * 2 * NT, P], [(DH + 1) * 2, NT], [2, 1]])
                if False else vbb.h[:, :].rearrange("pp (tt cc) -> pp tt cc", tt=NT)[:, :, DH:DH + 1],
                1.0), writes=[vbb])
        for dt in range(NDT):
            p.dma("pool", wpsb.h[:, dt * D:(dt + 1) * D],
                  wp_in[dt * P:(dt + 1) * P, :], writes=[wpsb])
        shiftc = c.constcol(SHIFT)
        for g in range(NQT // G):
            for u in range(G):
                qt = g * G + u
                p.dma("sp", qTg.h[:, :].rearrange("pp (i gq) -> pp i gq", i=NDT)[
                    :, :, u * P:(u + 1) * P], qT_blk[qt], writes=[qTg])
                p.dma("sp", pqg.h[:, :].rearrange("pp (i gq) -> pp i gq", i=NDT)[
                    :, :, u * P:(u + 1) * P], pq_blk[qt], writes=[pqg])
            for kt in range(NT):
                pb = preb[kt % 2]
                p.dma("sp", pb.h[:, :].rearrange("pp (i cc) -> pp i cc", i=NDT),
                      pre_blk[kt], writes=[pb])
                ps = c.psum_next()
                pairs = [(pb.h[:, dt * P:(dt + 1) * P],
                          pqg.h[:, dt * GQ:(dt + 1) * GQ]) for dt in range(NDT)]
                mm_group(p, ps, ps.h[:, 0:GQ], pairs, [pb, pqg])
                p.emit("ve", lambda h, ps=ps, kt=kt: h.tensor_copy(
                    SLg.h[:, kt * GQ:(kt + 1) * GQ], ps.h[:, 0:GQ]), reads=[ps], writes=[SLg])
            for pr in range(H // 2):
                kb = kTi[pr % 2]
                p.dma("sp", kb.h[:, :], kT_scr[pr * P:(pr + 1) * P, :], writes=[kb])
                for sub in range(2):
                    hh = pr * 2 + sub
                    r0 = sub * DH
                    for kt in range(NT):
                        ps = c.psum_next()

                        def fsc(h, ps=ps, kt=kt, r0=r0, kb=kb, pr=pr):
                            h.matmul(ps.h[:, 0:GQ],
                                     kb.h[r0:r0 + DH, kt * P:(kt + 1) * P],
                                     qTg.h[r0:r0 + DH, pr * GQ:(pr + 1) * GQ],
                                     start=True, stop=False)
                            return h.matmul(ps.h[:, 0:GQ], c.ident.h[:, :],
                                            SLg.h[:, kt * GQ:(kt + 1) * GQ],
                                            start=False, stop=True)
                        p.emit("te", fsc, reads=[kb, qTg, c.ident, SLg], writes=[ps])
                        p.emit("act", lambda h, ps=ps, kt=kt: h.activation(
                            Pg.h[:, kt * GQ:(kt + 1) * GQ], ps.h[:, 0:GQ], AF.Exp,
                            bias=shiftc.h[:, 0:1]), reads=[ps, shiftc], writes=[Pg])
                    vbb = vb65[hh % 2]
                    p.dma("sp",
                          vbb.h[:, :].rearrange("pp (tt cc) -> pp tt cc", tt=NT)[:, :, 0:DH],
                          v_scr.rearrange("(tt pp) d -> pp tt d", pp=P)[
                              :, :, hh * DH:(hh + 1) * DH],
                          writes=[vbb], allow_slow_non_contiguous=True)
                    psv = c.psum_next()

                    def fpv(h, vbb=vbb, psv=psv):
                        last = None
                        for kt in range(NT):
                            last = h.matmul(psv.h[0:DH + 1, 0:GQ],
                                            vbb.h[:, kt * (DH + 1):(kt + 1) * (DH + 1)],
                                            Pg.h[:, kt * GQ:(kt + 1) * GQ],
                                            start=(kt == 0), stop=(kt == NT - 1))
                        return last
                    p.emit("te", fpv, reads=[vbb, Pg], writes=[psv])
                    p.emit("ve", lambda h, psv=psv: h.tensor_copy(
                        lrow.h[0:1, :], psv.h[DH:DH + 1, 0:GQ]), reads=[psv], writes=[lrow])
                    p.emit("ve", lambda h: h.reciprocal(lrow.h[0:1, :], lrow.h[0:1, :]),
                           reads=[lrow], writes=[lrow])
                    psb2 = c.psum_next()
                    p.emit("te", lambda h, psb2=psb2: h.matmul(
                        psb2.h[0:DH, 0:GQ], c.ones_row_f.h[0:1, 0:DH], lrow.h[0:1, :],
                        start=True, stop=True), reads=[c.ones_row_f, lrow], writes=[psb2])
                    p.emit("ve", lambda h, psb2=psb2: h.tensor_copy(
                        rlbc.h[0:DH, :], psb2.h[0:DH, 0:GQ]), reads=[psb2], writes=[rlbc])
                    p.emit("ve", lambda h, psv=psv, pr=pr, r0=r0: h.tensor_tensor(
                        outTg.h[r0:r0 + DH, pr * GQ:(pr + 1) * GQ],
                        psv.h[0:DH, 0:GQ], rlbc.h[0:DH, :], OP.mult),
                        reads=[psv, rlbc], writes=[outTg])
            for u in range(G):
                qt = g * G + u
                p.dma("sp", xqt2.h[:, :], xq_in[qt * P:(qt + 1) * P, :], writes=[xqt2])
                emit_ln(p, c, xqt2, xqt2.h[:, :], xnf, xnf.h[:, :])
                for dc in range(2):
                    ps = c.psum_next()
                    pairs = [(outTg.h[:, i * GQ + u * P: i * GQ + (u + 1) * P],
                              wpsb.h[:, i * D + dc * 512: i * D + (dc + 1) * 512])
                             for i in range(NDT)]
                    pairs.append((c.ones_row.h[0:1, 0:P],
                                  bprow.h[0:1, dc * 512:(dc + 1) * 512]))
                    mm_group(p, ps, ps.h[:, 0:512], pairs, [outTg, wpsb, c.ones_row, bprow])
                    p.emit("ve", lambda h, ps=ps, dc=dc: h.tensor_add(
                        tmp2.h[:, dc * 512:(dc + 1) * 512], ps.h[:, 0:512],
                        xnf.h[:, dc * 512:(dc + 1) * 512]), reads=[ps, xnf], writes=[tmp2])
                p.emit("ve", lambda h: h.tensor_add(xo2.h[:, :], tmp2.h[:, :], xqt2.h[:, :]),
                       reads=[tmp2, xqt2], writes=[xo2])
                p.dma("sp", x_out[qt * P:(qt + 1) * P, :], xo2.h[:, :], reads=[xo2])
        p.close_scope()
        p.barrier()
        p.finalize()
    return nc



def build_l0attn_og():
    HD = 512
    NH = 4
    DK = 128
    SCALE = 1.0 / math.sqrt(128.0)
    nc = bass.Bass("TRN2", target_bir_lowering=False)
    xfull = nc.dram_tensor("xfull", [S, D], F32, kind="ExternalInput").ap()
    wq_in = nc.dram_tensor("Wq4", [HD, D], F32, kind="ExternalInput").ap()
    wk_in = nc.dram_tensor("Wk4", [HD, D], F32, kind="ExternalInput").ap()
    wv_in = nc.dram_tensor("Wv4", [HD, D], F32, kind="ExternalInput").ap()
    wg_in = nc.dram_tensor("Wg4", [HD, D], F32, kind="ExternalInput").ap()
    lw_in = nc.dram_tensor("lepe_w4", [HD, 5], F32, kind="ExternalInput").ap()
    lb_in = nc.dram_tensor("lepe_b4", [HD], F32, kind="ExternalInput").ap()
    og_out = nc.dram_tensor("og", [HD, S], BF16, kind="ExternalOutput").ap()

    gT_scr = nc.dram_tensor("gT_scr", [HD, S], BF16).ap()
    qr_scr = nc.dram_tensor("qr_scr", [HD, S], BF16).ap()
    NDT = D // P
    NT = S // P

    with ExitStack() as stack:
        p = Prog(nc, stack)
        c = Common(p, n_psum_f32=4)
        c.need_identf()

        lbcol = p.sbuf("lbcol", [P, NH], F32)
        emit_cols_from_1d(p, c, lb_in.rearrange("(r p) -> r p", p=P), NH,
                          lbcol, lbcol.h[:, 0:NH])
        wlep = p.sbuf("wlep", [P, NH * 5], F32)
        p.dma("pool", wlep.h[:, :].rearrange("p (a j) -> p a j", a=NH),
              lw_in.rearrange("(a p) j -> p a j", p=P), writes=[wlep],
              allow_slow_non_contiguous=True)

        krtm = p.sbuf("krtm", [P, NT * HD], BF16)
        vaug = p.sbuf("vaug", [P, NT * NH * (DK + 1)], BF16)
        VW = NH * (DK + 1)

        p.open_scope()
        xnT = p.sbuf("xnT0", [P, NDT * S], BF16)
        xstr = [p.sbuf("x0str0", [P, D], F32)]
        xn = p.sbuf("xn0", [P, D], BF16)
        for tt in range(NT):
            xb = xstr[0]
            p.dma("sp", xb.h[:, :], xfull[tt * P:(tt + 1) * P, :], writes=[xb])
            emit_ln(p, c, xb, xb.h[:, :], xn, xn.h[:, :])
            for b0 in range(0, NDT, 4):
                srcs = [xn.h[:, j * P:(j + 1) * P] for j in range(b0, b0 + 4)]
                dsts = [xnT.h[:, j * S + tt * P: j * S + (tt + 1) * P] for j in range(b0, b0 + 4)]
                emit_transpose4(p, c, srcs, dsts, [xn], [xnT])

        w4T = p.sbuf("w4T", [P, NDT * HD], BF16)
        wstg = p.sbuf("w0stg", [P, D], BF16)

        def load_w4T(w_hbm):
            for r in range(HD // P):
                p.dma("pool", wstg.h[:, :], w_hbm[r * P:(r + 1) * P, :], writes=[wstg])
                for b0 in range(0, NDT, 4):
                    srcs = [wstg.h[:, j * P:(j + 1) * P] for j in range(b0, b0 + 4)]
                    dsts = [w4T.h[:, j * HD + r * P: j * HD + (r + 1) * P]
                            for j in range(b0, b0 + 4)]
                    emit_transpose4(p, c, srcs, dsts, [wstg], [w4T])

        ropetmp = p.sbuf("ropetmp", [P, 512], F32)
        ropetmp2 = p.sbuf("ropetmp2", [P, 512], F32)
        krc = p.sbuf("krc", [P, 512], BF16)

        def rope_evac(ps, ch, d0, d1, dst_buf):
            cs = cos_t.h[:, ch * 512:(ch + 1) * 512]
            sn = sin_t.h[:, ch * 512:(ch + 1) * 512]
            p.emit("ve", lambda h: h.tensor_tensor(ropetmp.h[:, :], ps.h[:, 0:512], cs,
                                                   OP.mult), reads=[ps, cos_t], writes=[ropetmp])
            p.emit("ve", lambda h: h.tensor_tensor(
                ropetmp2.h[0:64, :], ps.h[64:128, 0:512], sn[0:64, :], OP.mult),
                reads=[ps, sin_t], writes=[ropetmp2])
            p.emit("ve", lambda h: h.tensor_tensor(
                ropetmp2.h[64:128, :], ps.h[0:64, 0:512], sn[64:128, :], OP.mult),
                reads=[ps, sin_t], writes=[ropetmp2])
            p.emit("ve", lambda h: h.tensor_sub(d0, ropetmp.h[0:64, :],
                                                ropetmp2.h[0:64, :]),
                   reads=[ropetmp, ropetmp2], writes=[dst_buf])
            p.emit("ve", lambda h: h.tensor_add(d1, ropetmp.h[64:128, :],
                                                ropetmp2.h[64:128, :]),
                   reads=[ropetmp, ropetmp2], writes=[dst_buf])

        p.open_scope()
        sin_t = p.sbuf("sin_t", [P, S], F32)
        cos_t = p.sbuf("cos_t", [P, S], F32)
        p.open_scope()
        invf = p.sbuf("invf", [P, 1], F32)
        t1c = p.sbuf("t1c", [P, 1], F32)
        t2c = p.sbuf("t2c", [P, 1], F32)
        ic32 = p.sbuf("ic32", [P, 1], I32)
        p.emit("pool", lambda h: h.iota(t1c.h[:, :], pattern=[[0, 1]], base=0,
                                        channel_multiplier=1,
                                        allow_small_or_imprecise_dtypes=True), writes=[t1c])
        p.emit("ve", lambda h: h.tensor_scalar(t2c.h[:, :], t1c.h[:, :], 64.0, None,
                                               OP.is_ge), reads=[t1c], writes=[t2c])
        p.emit("ve", lambda h: h.scalar_tensor_tensor(t1c.h[:, :], t2c.h[:, :], -64.0,
                                                      t1c.h[:, :], OP.mult, OP.add),
               reads=[t1c, t2c], writes=[t1c])
        C_LOG2 = math.log2(10000.0) / 64.0
        p.emit("ve", lambda h: h.tensor_scalar(t1c.h[:, :], t1c.h[:, :], -C_LOG2, None,
                                               OP.mult), reads=[t1c], writes=[t1c])
        p.emit("ve", lambda h: h.tensor_scalar(t2c.h[:, :], t1c.h[:, :], 0.5, None,
                                               OP.subtract), reads=[t1c], writes=[t2c])
        p.emit("ve", lambda h: h.tensor_copy(ic32.h[:, :], t2c.h[:, :]),
               reads=[t2c], writes=[ic32])
        p.emit("ve", lambda h: h.tensor_copy(t2c.h[:, :], ic32.h[:, :]),
               reads=[ic32], writes=[t2c])
        p.emit("ve", lambda h: h.tensor_sub(invf.h[:, :], t1c.h[:, :], t2c.h[:, :]),
               reads=[t1c, t2c], writes=[invf])
        LN2 = math.log(2.0)
        coef = [LN2 ** j / math.factorial(j) for j in range(9)]
        p.emit("ve", lambda h: h.tensor_scalar(t1c.h[:, :], invf.h[:, :], coef[8],
                                               coef[7], OP.mult, OP.add),
               reads=[invf], writes=[t1c])
        for j in range(6, -1, -1):
            p.emit("ve", lambda h, j=j: h.scalar_tensor_tensor(
                t1c.h[:, :], t1c.h[:, :], 1.0, invf.h[:, :], OP.mult, OP.mult),
                reads=[t1c, invf], writes=[t1c])
            p.emit("ve", lambda h, j=j: h.tensor_scalar(
                t1c.h[:, :], t1c.h[:, :], coef[j], None, OP.add), reads=[t1c], writes=[t1c])
        p.emit("ve", lambda h: h.tensor_scalar(ic32.h[:, :], ic32.h[:, :], 127, None,
                                               OP.add), reads=[ic32], writes=[ic32])
        p.emit("ve", lambda h: h.tensor_scalar(ic32.h[:, :], ic32.h[:, :], 23, None,
                                               OP.arith_shift_left), reads=[ic32], writes=[ic32])
        p.emit("ve", lambda h: h.tensor_mul(invf.h[:, :], t1c.h[:, :],
                                            ic32.h[:, :].bitcast(F32)),
               reads=[t1c, ic32], writes=[invf])
        HS = S // 4
        theta = p.sbuf("theta", [P, HS], F32)
        kfb = p.sbuf("kfb", [P, HS], F32)
        C1 = 6.28125
        C2 = 0.0019353071795864769
        INV2PI = 1.0 / (2.0 * math.pi)
        for half in range(4):
            p.emit("pool", lambda h, half=half: h.iota(
                theta.h[:, :], pattern=[[1, HS]], base=half * HS, channel_multiplier=0,
                allow_small_or_imprecise_dtypes=True), writes=[theta])
            p.emit("ve", lambda h: h.tensor_scalar(theta.h[:, :], theta.h[:, :],
                                                   invf.h[:, 0:1], None, OP.mult),
                   reads=[theta, invf], writes=[theta])
            for dst, extra in ((sin_t, 0.0), (cos_t, 0.25)):
                dsl = dst.h[:, half * HS:(half + 1) * HS]
                p.emit("ve", lambda h, extra=extra: h.tensor_scalar(
                    kfb.h[:, :], theta.h[:, :], INV2PI, extra, OP.mult, OP.add),
                    reads=[theta], writes=[kfb])
                p.emit("ve", lambda h, dsl=dsl, dst=dst: h.tensor_copy(
                    dsl.bitcast(I32), kfb.h[:, :]), reads=[kfb], writes=[dst])
                p.emit("ve", lambda h, dsl=dsl, dst=dst: h.tensor_copy(
                    kfb.h[:, :], dsl.bitcast(I32)), reads=[dst], writes=[kfb])
                p.emit("ve", lambda h, dsl=dsl, dst=dst: h.scalar_tensor_tensor(
                    dsl, kfb.h[:, :], -C1, theta.h[:, :], OP.mult, OP.add),
                    reads=[kfb, theta], writes=[dst])
                p.emit("ve", lambda h, dsl=dsl, dst=dst: h.scalar_tensor_tensor(
                    dsl, kfb.h[:, :], -C2, dsl, OP.mult, OP.add),
                    reads=[kfb, dst], writes=[dst])
        hpi = c.constcol(math.pi / 2.0)
        p.emit("act", lambda h: h.activation(sin_t.h[:, :], sin_t.h[:, :], AF.Sin),
               reads=[sin_t], writes=[sin_t])
        p.emit("act", lambda h: h.activation(cos_t.h[:, :], cos_t.h[:, :], AF.Sin,
                                             bias=hpi.h[:, 0:1]),
               reads=[cos_t, hpi], writes=[cos_t])
        p.close_scope()

        load_w4T(wq_in)
        for hh in range(NH):
            for ch in range(NT // 4):
                ps = c.psum_next()
                pairs = [(w4T.h[:, j * HD + hh * DK: j * HD + (hh + 1) * DK],
                          xnT.h[:, j * S + ch * 512: j * S + (ch + 1) * 512])
                         for j in range(NDT)]
                mm_group(p, ps, ps.h[:, 0:512], pairs, [w4T, xnT])
                rope_evac(ps, ch, krc.h[0:64, :], krc.h[64:128, :], krc)
                p.dma("sp", qr_scr[hh * P:(hh + 1) * P, ch * 512:(ch + 1) * 512],
                      krc.h[:, :], reads=[krc])
        load_w4T(wk_in)
        for hh in range(NH):
            for ch in range(NT // 4):
                ps = c.psum_next()
                pairs = [(w4T.h[:, j * HD + hh * DK: j * HD + (hh + 1) * DK],
                          xnT.h[:, j * S + ch * 512: j * S + (ch + 1) * 512])
                         for j in range(NDT)]
                mm_group(p, ps, ps.h[:, 0:512], pairs, [w4T, xnT])
                rope_evac(ps, ch, krc.h[0:64, :], krc.h[64:128, :], krc)
                srcs = [krc.h[:, u * P:(u + 1) * P] for u in range(4)]
                dsts = [krtm.h[:, (ch * 4 + u) * HD + hh * DK:
                               (ch * 4 + u) * HD + (hh + 1) * DK] for u in range(4)]
                emit_transpose4(p, c, srcs, dsts, [krc], [krtm])
        p.close_scope()
        load_w4T(wv_in)
        p.emit("pool", lambda h: h.memset(
            vaug.h[:, :].rearrange("p (tt hh cc) -> p tt hh cc", tt=NT, hh=NH)[
                :, :, :, DK:DK + 1], 1.0), writes=[vaug])
        for tt in range(NT):
            ps = c.psum_next()
            pairs = [(xnT.h[:, j * S + tt * P: j * S + (tt + 1) * P],
                      w4T.h[:, j * HD:(j + 1) * HD]) for j in range(NDT)]
            mm_group(p, ps, ps.h[:, 0:HD], pairs, [xnT, w4T])
            for hh in range(NH):
                p.emit("ve", lambda h, ps=ps, tt=tt, hh=hh: h.tensor_copy(
                    vaug.h[:, tt * VW + hh * (DK + 1): tt * VW + hh * (DK + 1) + DK],
                    ps.h[:, hh * DK:(hh + 1) * DK]), reads=[ps], writes=[vaug])
        load_w4T(wg_in)
        gstg = p.sbuf("gstg", [P, 512], BF16)
        for r in range(NH):
            for ch in range(NT // 4):
                ps = c.psum_next()
                pairs = [(w4T.h[:, j * HD + r * DK: j * HD + (r + 1) * DK],
                          xnT.h[:, j * S + ch * 512: j * S + (ch + 1) * 512])
                         for j in range(NDT)]
                mm_group(p, ps, ps.h[:, 0:512], pairs, [w4T, xnT])
                p.emit("ve", lambda h, ps=ps: h.tensor_copy(gstg.h[:, :], ps.h[:, 0:512]),
                       reads=[ps], writes=[gstg])
                p.dma("sp", gT_scr[r * P:(r + 1) * P, ch * 512:(ch + 1) * 512],
                      gstg.h[:, :], reads=[gstg])
        p.close_scope()

        p.open_scope()
        kvsb = p.sbuf("kvsb", [P, DK + 1], BF16)
        ksumc = p.sbuf("ksumc", [P, 1], BF16)
        vTs = p.sbuf("vTs", [P, S], BF16)
        lepeT = p.sbuf("lepeT", [P, S], F32)
        vsumc = p.sbuf("vsumc", [P, 1], F32)
        denr = p.sbuf("denr", [1, 512], F32)
        rdenbc = p.sbuf("rdenbc", [P, 512], F32)
        otmp = p.sbuf("otmp", [P, 512], F32)
        gstr = [p.sbuf(f"gstr{i}", [P, 512], BF16) for i in range(2)]
        qstr = [p.sbuf(f"qstr{i}", [P, 512], BF16) for i in range(2)]
        ostg = p.sbuf("ostg", [P, 512], BF16)
        for hh in range(NH):
            psk = c.psum_next()

            def fkv(h, psk=psk, hh=hh):
                last = None
                for tt in range(NT):
                    last = h.matmul(psk.h[:, 0:DK + 1],
                                    krtm.h[:, tt * HD + hh * DK: tt * HD + (hh + 1) * DK],
                                    vaug.h[:, tt * VW + hh * (DK + 1):
                                           tt * VW + (hh + 1) * (DK + 1)],
                                    start=(tt == 0), stop=(tt == NT - 1))
                return last
            p.emit("te", fkv, reads=[krtm, vaug], writes=[psk])
            p.emit("ve", lambda h, psk=psk: h.tensor_copy(kvsb.h[:, :], psk.h[:, 0:DK + 1]),
                   reads=[psk], writes=[kvsb])
            p.emit("ve", lambda h, psk=psk: h.tensor_copy(ksumc.h[:, :],
                                                          psk.h[:, DK:DK + 1]),
                   reads=[psk], writes=[ksumc])
            for b0 in range(0, NT, 4):
                srcs = [vaug.h[:, tt * VW + hh * (DK + 1): tt * VW + hh * (DK + 1) + DK]
                        for tt in range(b0, b0 + 4)]
                dsts = [vTs.h[:, tt * P:(tt + 1) * P] for tt in range(b0, b0 + 4)]
                emit_transpose4(p, c, srcs, dsts, [vaug], [vTs])
            p.emit("ve", lambda h: h.tensor_reduce(vsumc.h[:, :], vTs.h[:, :], axis=AX.X,
                                                   op=OP.add), reads=[vTs], writes=[vsumc])
            p.emit("ve", lambda h, hh=hh: h.tensor_scalar(
                lepeT.h[:, :], vTs.h[:, :], wlep.h[:, hh * 5 + 2: hh * 5 + 3],
                lbcol.h[:, hh:hh + 1], OP.mult, OP.add),
                reads=[vTs, wlep, lbcol], writes=[lepeT])
            for tau, jj in ((-2, 0), (-1, 1), (1, 3), (2, 4)):
                lo, hi = max(0, tau), S + min(0, tau)

                def flep(h, tau=tau, jj=jj, lo=lo, hi=hi, hh=hh):
                    return h.scalar_tensor_tensor(
                        lepeT.h[:, lo:hi], vTs.h[:, lo - tau:hi - tau],
                        wlep.h[:, hh * 5 + jj: hh * 5 + jj + 1],
                        lepeT.h[:, lo:hi], OP.mult, OP.add)
                p.emit("ve", flep, reads=[vTs, wlep, lepeT], writes=[lepeT])
            for ch in range(NT // 4):
                gsb = gstr[ch % 2]
                p.dma("sp", gsb.h[:, :], gT_scr[hh * P:(hh + 1) * P, ch * 512:(ch + 1) * 512],
                      writes=[gsb])
                qsb = qstr[ch % 2]
                p.dma("sp", qsb.h[:, :], qr_scr[hh * P:(hh + 1) * P, ch * 512:(ch + 1) * 512],
                      writes=[qsb])
                psn = c.psum_next()
                p.emit("te", lambda h, psn=psn, qsb=qsb: h.matmul(
                    psn.h[:, 0:512], kvsb.h[:, 0:DK], qsb.h[:, :],
                    start=True, stop=True), reads=[kvsb, qsb], writes=[psn])
                psd = c.psum_next()
                p.emit("te", lambda h, psd=psd, qsb=qsb: h.matmul(
                    psd.h[0:1, 0:512], ksumc.h[:, 0:1], qsb.h[:, :],
                    start=True, stop=True), reads=[ksumc, qsb], writes=[psd])
                p.emit("ve", lambda h, psd=psd: h.tensor_scalar(
                    denr.h[0:1, :], psd.h[0:1, 0:512], SCALE, float(S), OP.mult, OP.add),
                    reads=[psd], writes=[denr])
                p.emit("ve", lambda h: h.reciprocal(denr.h[0:1, :], denr.h[0:1, :]),
                       reads=[denr], writes=[denr])
                psb = c.psum_next()
                p.emit("te", lambda h, psb=psb: h.matmul(
                    psb.h[:, 0:512], c.ones_row_f.h[0:1, :], denr.h[0:1, :],
                    start=True, stop=True), reads=[c.ones_row_f, denr], writes=[psb])
                p.emit("ve", lambda h, psb=psb: h.tensor_copy(rdenbc.h[:, :],
                                                              psb.h[:, 0:512]),
                       reads=[psb], writes=[rdenbc])
                p.emit("act", lambda h, psn=psn: h.activation(
                    otmp.h[:, :], psn.h[:, 0:512], AF.Identity, scale=SCALE,
                    bias=vsumc.h[:, 0:1]), reads=[psn, vsumc], writes=[otmp])
                p.emit("ve", lambda h: h.tensor_mul(otmp.h[:, :], otmp.h[:, :],
                                                    rdenbc.h[:, :]),
                       reads=[otmp, rdenbc], writes=[otmp])
                p.emit("ve", lambda h, ch=ch: h.tensor_add(
                    otmp.h[:, :], otmp.h[:, :], lepeT.h[:, ch * 512:(ch + 1) * 512]),
                    reads=[otmp, lepeT], writes=[otmp])
                p.emit("ve", lambda h, gsb=gsb: h.tensor_mul(ostg.h[:, :], otmp.h[:, :],
                                                             gsb.h[:, :]),
                       reads=[otmp, gsb], writes=[ostg])
                p.dma("sp", og_out[hh * P:(hh + 1) * P, ch * 512:(ch + 1) * 512],
                      ostg.h[:, :], reads=[ostg])
        p.close_scope()
        p.barrier()
        p.finalize()
    return nc


def build_l0attn_out(T=T_LOC):
    nc = bass.Bass("TRN2", target_bir_lowering=False)
    ogT_in = nc.dram_tensor("ogT", [D, T], BF16, kind="ExternalInput").ap()
    wo_in = nc.dram_tensor("Wo", [D, D], F32, kind="ExternalInput").ap()
    x_in = nc.dram_tensor("x", [T, D], F32, kind="ExternalInput").ap()
    x_out = nc.dram_tensor("xo", [T, D], F32, kind="ExternalOutput").ap()
    NDT = D // P
    NTT = T // P
    with ExitStack() as stack:
        p = Prog(nc, stack)
        c = Common(p)
        c.need_identf()
        ogT = p.sbuf("ogTsb", [P, NDT * T], BF16)
        for i in range(NDT):
            p.dma("sp", ogT.h[:, i * T:(i + 1) * T], ogT_in[i * P:(i + 1) * P, :],
                  writes=[ogT])
        woT = p.sbuf("woT", [P, NDT * D], BF16)
        wstg = p.sbuf("wostg", [P, D], BF16)
        for r in range(NDT):
            p.dma("pool", wstg.h[:, :], wo_in[r * P:(r + 1) * P, :], writes=[wstg])
            for b0 in range(0, NDT, 4):
                srcs = [wstg.h[:, j * P:(j + 1) * P] for j in range(b0, b0 + 4)]
                dsts = [woT.h[:, j * D + r * P: j * D + (r + 1) * P] for j in range(b0, b0 + 4)]
                emit_transpose4(p, c, srcs, dsts, [wstg], [woT])
        xb = p.sbuf("xol_x", [P, D], F32)
        xo = p.sbuf("xol_o", [P, D], F32)
        for tt in range(NTT):
            p.dma("sp", xb.h[:, :], x_in[tt * P:(tt + 1) * P, :], writes=[xb])
            for dc in range(2):
                ps = c.psum_next()
                pairs = [(ogT.h[:, i * T + tt * P: i * T + (tt + 1) * P],
                          woT.h[:, i * D + dc * 512: i * D + (dc + 1) * 512])
                         for i in range(NDT)]
                mm_group(p, ps, ps.h[:, 0:512], pairs, [ogT, woT])
                p.emit("ve", lambda h, ps=ps, dc=dc: h.scalar_tensor_tensor(
                    xo.h[:, dc * 512:(dc + 1) * 512], ps.h[:, 0:512], 1.0,
                    xb.h[:, dc * 512:(dc + 1) * 512], OP.mult, OP.add),
                    reads=[ps, xb], writes=[xo])
            p.dma("sp", x_out[tt * P:(tt + 1) * P, :], xo.h[:, :], reads=[xo])
        p.barrier()
        p.finalize()
    return nc



def build_l3gate(T=T_LOC):
    nc = bass.Bass("TRN2", target_bir_lowering=False)
    x_in = nc.dram_tensor("x", [T, D], F32, kind="ExternalInput").ap()
    wg_in = nc.dram_tensor("Wg", [D, D], F32, kind="ExternalInput").ap()
    bg_in = nc.dram_tensor("bg", [D], F32, kind="ExternalInput").ap()
    xg_out = nc.dram_tensor("xg", [T, D], F32, kind="ExternalOutput").ap()
    NDT = D // P
    NTT = T // P
    with ExitStack() as stack:
        p = Prog(nc, stack)
        c = Common(p)
        c.need_identf()
        bgrow = p.sbuf("bgrow", [1, D], BF16)
        p.dma("pool", bgrow.h[:, :], bg_in.unsqueeze(0), writes=[bgrow])
        wgT = p.sbuf("wgT", [P, NDT * D], BF16)
        wstg = p.sbuf("wg_stg", [P, D], BF16)
        for r in range(NDT):
            p.dma("pool", wstg.h[:, :], wg_in[r * P:(r + 1) * P, :], writes=[wstg])
            for b0 in range(0, NDT, 4):
                srcs = [wstg.h[:, j * P:(j + 1) * P] for j in range(b0, b0 + 4)]
                dsts = [wgT.h[:, j * D + r * P: j * D + (r + 1) * P] for j in range(b0, b0 + 4)]
                emit_transpose4(p, c, srcs, dsts, [wstg], [wgT])
        xb = p.sbuf("xg_x", [P, D], F32)
        xnf = p.sbuf("xg_xnf", [P, D], F32)
        xn = p.sbuf("xg_xn", [P, D], BF16)
        xnT = p.sbuf("xg_xnT", [P, NDT * P], BF16)
        gate = p.sbuf("xg_gate", [P, D], F32)
        for tt in range(NTT):
            p.dma("sp", xb.h[:, :], x_in[tt * P:(tt + 1) * P, :], writes=[xb])
            emit_ln(p, c, xb, xb.h[:, :], xnf, xnf.h[:, :])
            p.emit("ve", lambda h: h.tensor_copy(xn.h[:, :], xnf.h[:, :]),
                   reads=[xnf], writes=[xn])
            for b0 in range(0, NDT, 4):
                srcs = [xn.h[:, j * P:(j + 1) * P] for j in range(b0, b0 + 4)]
                dsts = [xnT.h[:, j * P:(j + 1) * P] for j in range(b0, b0 + 4)]
                emit_transpose4(p, c, srcs, dsts, [xn], [xnT])
            for dc in range(2):
                ps = c.psum_next()
                pairs = [(xnT.h[:, j * P:(j + 1) * P],
                          wgT.h[:, j * D + dc * 512: j * D + (dc + 1) * 512])
                         for j in range(NDT)]
                pairs.append((c.ones_row.h[0:1, 0:P], bgrow.h[0:1, dc * 512:(dc + 1) * 512]))
                mm_group(p, ps, ps.h[:, 0:512], pairs, [xnT, wgT, c.ones_row, bgrow])
                p.emit("act", lambda h, ps=ps, dc=dc: h.activation(
                    gate.h[:, dc * 512:(dc + 1) * 512], ps.h[:, 0:512], AF.Sigmoid),
                    reads=[ps], writes=[gate])
            p.emit("ve", lambda h: h.tensor_mul(gate.h[:, :], gate.h[:, :], xnf.h[:, :]),
                   reads=[gate, xnf], writes=[gate])
            p.dma("sp", xg_out[tt * P:(tt + 1) * P, :], gate.h[:, :], reads=[gate])
        p.barrier()
        p.finalize()
    return nc


def build_l3ssm():
    CH = 512
    N = 16
    CK = 32
    NCHK = S // CK
    NST = CH // P
    nc = bass.Bass("TRN2", target_bir_lowering=False)
    xg_in = nc.dram_tensor("xg", [S, CH], F32, kind="ExternalInput").ap()
    a_in = nc.dram_tensor("A4", [CH, N * N], F32, kind="ExternalInput").ap()
    bp_in = nc.dram_tensor("Bp4", [CH, N], F32, kind="ExternalInput").ap()
    cp_in = nc.dram_tensor("Cp4", [CH, N], F32, kind="ExternalInput").ap()
    dp_in = nc.dram_tensor("Dp4", [CH], F32, kind="ExternalInput").ap()
    y_out = nc.dram_tensor("yT", [CH, S], F32, kind="ExternalOutput").ap()

    with ExitStack() as stack:
        p = Prog(nc, stack)
        c = Common(p, n_psum_f32=4)
        c.need_identf()
        Asb = p.sbuf("Asb", [P, N * N], F32)
        Bpc = p.sbuf("Bpc", [P, N], F32)
        Cpc = p.sbuf("Cpc", [P, N], F32)
        Dpc = p.sbuf("Dpc", [P, 1], F32)
        utab = p.sbuf("utab", [P, CK * N], F32)
        vtab = p.sbuf("vtab", [P, CK * N], F32)
        ktab = p.sbuf("ktab", [P, CK], F32)
        Msb = p.sbuf("Msb", [P, N * N], F32)
        MV = p.sbuf("MV", [P, (N + CK) * N], F32)
        tmp6 = p.sbuf("tmp6", [P, N * N * N], F32)
        scur = p.sbuf("scur", [P, N], F32)
        snew = p.sbuf("snew", [P, N + CK], F32)
        xgT = p.sbuf("xgT3", [P, S], F32)
        yT = p.sbuf("yT3", [P, S], F32)
        xstg = p.sbuf("xstg3", [P, CH], F32)
        for st_ in range(NST):
            c0 = st_ * P
            p.dma("sp", Asb.h[:, :], a_in[c0:c0 + P, :], writes=[Asb])
            p.dma("sp", Bpc.h[:, :], bp_in[c0:c0 + P, :], writes=[Bpc])
            p.dma("sp", Cpc.h[:, :], cp_in[c0:c0 + P, :], writes=[Cpc])
            p.dma("sp", Dpc.h[:, :], dp_in[c0:c0 + P].unsqueeze(1), writes=[Dpc])
            p.emit("ve", lambda h: h.tensor_copy(utab.h[:, 0:N], Bpc.h[:, :]),
                   reads=[Bpc], writes=[utab])
            A_mn = Asb.h[:, 0:N * N].rearrange("p (n m) -> p m n", n=N)
            for j in range(CK - 1):
                uj = utab.h[:, j * N:(j + 1) * N].unsqueeze(1).broadcast_to((P, N, N))
                p.emit("ve", lambda h, uj=uj: h.tensor_tensor(
                    tmp6.h[:, 0:N * N].rearrange("p (m n) -> p m n", n=N), uj, A_mn,
                    OP.mult), reads=[utab, Asb], writes=[tmp6])
                p.emit("ve", lambda h, j=j: h.tensor_reduce(
                    utab.h[:, (j + 1) * N:(j + 2) * N],
                    tmp6.h[:, 0:N * N].rearrange("p (m n) -> p m n", n=N),
                    axis=AX.X, op=OP.add), reads=[tmp6], writes=[utab])
            cpb = Cpc.h[:, :].unsqueeze(1).broadcast_to((P, CK, N))
            p.emit("ve", lambda h: h.tensor_tensor(
                tmp6.h[:, 0:CK * N].rearrange("p (j n) -> p j n", n=N),
                utab.h[:, :].rearrange("p (j n) -> p j n", n=N), cpb, OP.mult),
                reads=[utab, Cpc], writes=[tmp6])
            p.emit("ve", lambda h: h.tensor_reduce(
                ktab.h[:, :], tmp6.h[:, 0:CK * N].rearrange("p (j n) -> p j n", n=N),
                axis=AX.X, op=OP.add), reads=[tmp6], writes=[ktab])
            p.emit("ve", lambda h: h.tensor_scalar(
                ktab.h[:, 0:1], ktab.h[:, 0:1], Dpc.h[:, 0:1], 1.0, OP.add, OP.add),
                reads=[ktab, Dpc], writes=[ktab])
            A_nm = Asb.h[:, 0:N * N].rearrange("p (n m) -> p n m", n=N)
            p.emit("ve", lambda h: h.tensor_tensor(
                tmp6.h[:, 0:N * N].rearrange("p (n m) -> p n m", n=N),
                Cpc.h[:, :].unsqueeze(1).broadcast_to((P, N, N)), A_nm, OP.mult),
                reads=[Cpc, Asb], writes=[tmp6])
            p.emit("ve", lambda h: h.tensor_reduce(
                vtab.h[:, 0:N], tmp6.h[:, 0:N * N].rearrange("p (n m) -> p n m", n=N),
                axis=AX.X, op=OP.add), reads=[tmp6], writes=[vtab])
            for i in range(CK - 1):
                vi = vtab.h[:, i * N:(i + 1) * N].unsqueeze(1).broadcast_to((P, N, N)).\
                    rearrange("p a b -> p b a")
                p.emit("ve", lambda h, vi=vi: h.tensor_tensor(
                    tmp6.h[:, 0:N * N].rearrange("p (n m) -> p n m", n=N), vi, A_nm,
                    OP.mult), reads=[vtab, Asb], writes=[tmp6])
                p.emit("ve", lambda h, i=i: h.tensor_reduce(
                    vtab.h[:, (i + 1) * N:(i + 2) * N],
                    tmp6.h[:, 0:N * N].rearrange("p (n m) -> p n m", n=N),
                    axis=AX.X, op=OP.add), reads=[tmp6], writes=[vtab])
            p.emit("ve", lambda h: h.tensor_copy(Msb.h[:, :], Asb.h[:, :]),
                   reads=[Asb], writes=[Msb])
            for _ in range(5):
                m_nl = Msb.h[:, :].rearrange("p (n l) -> p n l", n=N).unsqueeze(2).\
                    broadcast_to((P, N, N, N))
                m_lm = Msb.h[:, :].rearrange("p (l m) -> p l m", l=N).unsqueeze(1).\
                    broadcast_to((P, N, N, N))
                p1 = Msb.h[:, :].rearrange("p (n l) -> p n l", n=N)
                mA = bass.AP(p1.tensor, p1.offset,
                             [p1.ap[0], p1.ap[1], [0, N], p1.ap[2]])
                p2 = Msb.h[:, :].rearrange("p (l m) -> p l m", l=N)
                mB = bass.AP(p2.tensor, p2.offset,
                             [p2.ap[0], [0, N], p2.ap[2], p2.ap[1]])
                p.emit("ve", lambda h, mA=mA, mB=mB: h.tensor_tensor(
                    tmp6.h[:, 0:N * N * N].rearrange("p (n m l) -> p n m l", n=N, m=N),
                    mA, mB, OP.mult), reads=[Msb], writes=[tmp6])
                p.emit("ve", lambda h: h.tensor_reduce(
                    Msb.h[:, :].rearrange("p (n m) -> p n m", n=N),
                    tmp6.h[:, 0:N * N * N].rearrange("p (n m l) -> p n m l", n=N, m=N),
                    axis=AX.X, op=OP.add), reads=[tmp6], writes=[Msb])
            p.emit("ve", lambda h: h.tensor_copy(
                MV.h[:, 0:N * N].rearrange("p (m n) -> p m n", n=N),
                Msb.h[:, :].rearrange("p (n m) -> p m n", n=N)), reads=[Msb], writes=[MV])
            p.emit("ve", lambda h: h.tensor_copy(MV.h[:, N * N:(N + CK) * N],
                                                 vtab.h[:, :]), reads=[vtab], writes=[MV])
            for tt in range(S // P):
                p.dma("sp", xstg.h[:, :], xg_in[tt * P:(tt + 1) * P, :], writes=[xstg])
                ps = c.psum_next()
                p.emit("te", lambda h, ps=ps, st_=st_: h.transpose(
                    ps.h[:, 0:P], xstg.h[:, st_ * P:(st_ + 1) * P], c.ident_f.h[:, :]),
                    reads=[xstg, c.ident_f], writes=[ps])
                p.emit("ve", lambda h, ps=ps, tt=tt: h.tensor_copy(
                    xgT.h[:, tt * P:(tt + 1) * P], ps.h[:, 0:P]), reads=[ps], writes=[xgT])
            xg3 = xgT.h[:, :].rearrange("p (ch i) -> p ch i", i=CK)
            y3 = yT.h[:, :].rearrange("p (ch i) -> p ch i", i=CK)
            p.emit("ve", lambda h: h.tensor_scalar(
                yT.h[:, :], xgT.h[:, :], ktab.h[:, 0:1], None, OP.mult),
                reads=[xgT, ktab], writes=[yT])
            for tau in range(1, CK):
                def fintra(h, tau=tau):
                    return h.scalar_tensor_tensor(
                        y3[:, :, tau:CK], xg3[:, :, 0:CK - tau], ktab.h[:, tau:tau + 1],
                        y3[:, :, tau:CK], OP.mult, OP.add)
                p.emit("ve", fintra, reads=[xgT, ktab, yT], writes=[yT])
            Eb = p.sbuf(f"Eb3_{st_}", [P, NCHK * N], F32) if st_ == 0 else Eb
            if st_ == 0:
                Etmp = p.sbuf("Etmp3", [P, NCHK * N], F32)
            e3 = Eb.h[:, :].rearrange("p (ch m) -> p ch m", m=N)
            et3 = Etmp.h[:, :].rearrange("p (ch m) -> p ch m", m=N)
            for j in range(CK):
                xgj = bass.AP(xg3.tensor, xg3.offset + j * xg3.ap[2][0],
                              [xg3.ap[0], xg3.ap[1], [0, N]])
                urev = utab.h[:, (CK - 1 - j) * N:(CK - j) * N].unsqueeze(1).\
                    broadcast_to((P, NCHK, N))
                if j == 0:
                    p.emit("ve", lambda h, xgj=xgj, urev=urev: h.tensor_tensor(
                        e3, xgj, urev, OP.mult), reads=[xgT, utab], writes=[Eb])
                else:
                    p.emit("ve", lambda h, xgj=xgj, urev=urev: h.tensor_tensor(
                        et3, xgj, urev, OP.mult), reads=[xgT, utab], writes=[Etmp])
                    p.emit("ve", lambda h: h.tensor_add(Eb.h[:, :], Eb.h[:, :],
                                                        Etmp.h[:, :]),
                           reads=[Eb, Etmp], writes=[Eb])
            p.emit("pool", lambda h: h.memset(scur.h[:, :], 0.0), writes=[scur])
            mv3 = MV.h[:, :].rearrange("p (cc n) -> p cc n", n=N)
            for k in range(NCHK):
                sbc = scur.h[:, :].unsqueeze(1).broadcast_to((P, N + CK, N))
                p.emit("ve", lambda h, sbc=sbc: h.tensor_tensor(
                    tmp6.h[:, 0:(N + CK) * N].rearrange("p (cc n) -> p cc n", n=N),
                    sbc, mv3, OP.mult), reads=[scur, MV], writes=[tmp6])
                p.emit("ve", lambda h: h.tensor_reduce(
                    snew.h[:, :],
                    tmp6.h[:, 0:(N + CK) * N].rearrange("p (cc n) -> p cc n", n=N),
                    axis=AX.X, op=OP.add), reads=[tmp6], writes=[snew])
                p.emit("ve", lambda h, k=k: h.tensor_add(
                    yT.h[:, k * CK:(k + 1) * CK], yT.h[:, k * CK:(k + 1) * CK],
                    snew.h[:, N:N + CK]), reads=[yT, snew], writes=[yT])
                p.emit("ve", lambda h, k=k: h.tensor_add(
                    scur.h[:, :], snew.h[:, 0:N], Eb.h[:, k * N:(k + 1) * N]),
                    reads=[snew, Eb], writes=[scur])
            p.dma("sp", y_out[c0:c0 + P, :], yT.h[:, :], reads=[yT])
        p.barrier()
        p.finalize()
    return nc


def build_l3fin(T=T_LOC):
    nc = bass.Bass("TRN2", target_bir_lowering=False)
    y_in = nc.dram_tensor("y", [T, D], F32, kind="ExternalInput").ap()
    x_in = nc.dram_tensor("x", [T, D], F32, kind="ExternalInput").ap()
    gm_in = nc.dram_tensor("gamma", [D], F32, kind="ExternalInput").ap()
    bt_in = nc.dram_tensor("beta", [D], F32, kind="ExternalInput").ap()
    x_out = nc.dram_tensor("xo", [T, D], F32, kind="ExternalOutput").ap()
    NTT = T // P
    with ExitStack() as stack:
        p = Prog(nc, stack)
        c = Common(p)
        gmr = p.sbuf("gmr", [1, D], F32)
        btr = p.sbuf("btr", [1, D], F32)
        p.dma("sp", gmr.h[:, :], gm_in.unsqueeze(0), writes=[gmr])
        p.dma("sp", btr.h[:, :], bt_in.unsqueeze(0), writes=[btr])
        gmbc = p.sbuf("gmbc", [P, D], F32)
        btbc = p.sbuf("btbc", [P, D], F32)
        for row, bc in ((gmr, gmbc), (btr, btbc)):
            for dc in range(2):
                ps = c.psum_next()
                p.emit("te", lambda h, ps=ps, row=row, dc=dc: h.matmul(
                    ps.h[:, 0:512], c.ones_row_f.h[0:1, :],
                    row.h[0:1, dc * 512:(dc + 1) * 512], start=True, stop=True),
                    reads=[c.ones_row_f, row], writes=[ps])
                p.emit("ve", lambda h, ps=ps, bc=bc, dc=dc: h.tensor_copy(
                    bc.h[:, dc * 512:(dc + 1) * 512], ps.h[:, 0:512]),
                    reads=[ps], writes=[bc])
        yb = p.sbuf("fin_y", [P, D], F32)
        xb = p.sbuf("fin_x", [P, D], F32)
        yn = p.sbuf("fin_yn", [P, D], F32)
        for tt in range(NTT):
            p.dma("sp", yb.h[:, :], y_in[tt * P:(tt + 1) * P, :], writes=[yb])
            p.dma("sp", xb.h[:, :], x_in[tt * P:(tt + 1) * P, :], writes=[xb])
            emit_ln(p, c, yb, yb.h[:, :], yn, yn.h[:, :])
            p.emit("ve", lambda h: h.tensor_mul(yn.h[:, :], yn.h[:, :], gmbc.h[:, :]),
                   reads=[yn, gmbc], writes=[yn])
            p.emit("ve", lambda h: h.tensor_add(yn.h[:, :], yn.h[:, :], btbc.h[:, :]),
                   reads=[yn, btbc], writes=[yn])
            p.emit("ve", lambda h: h.tensor_add(yn.h[:, :], yn.h[:, :], xb.h[:, :]),
                   reads=[yn, xb], writes=[yn])
            p.dma("sp", x_out[tt * P:(tt + 1) * P, :], yn.h[:, :], reads=[yn])
        p.barrier()
        p.finalize()
    return nc



_PROGS = {}


def _prog(name, builder):
    if name not in _PROGS:
        _PROGS[name] = builder()
    return _PROGS[name]


def _run(nc, in_maps, out_key):
    res = run_bass_kernel_spmd(nc, in_maps, core_ids=list(range(N_CORES)))
    return [res.results[i][out_key] for i in range(N_CORES)]


def _tok_shards(x):
    return [np.ascontiguousarray(x[c // 2, (c % 2) * T_LOC:(c % 2 + 1) * T_LOC, :])
            for c in range(N_CORES)]


def _tok_gather(shards, x_like):
    out = np.empty_like(x_like)
    for c in range(N_CORES):
        out[c // 2, (c % 2) * T_LOC:(c % 2 + 1) * T_LOC, :] = shards[c]
    return out


def _ffn_phase(x, name, builder, wmap):
    nc = _prog(name, builder)
    shards = _tok_shards(x)
    in_maps = [{"x": s, **wmap} for s in shards]
    return _tok_gather(_run(nc, in_maps, "xo"), x)


def kernel(**inputs):
    x = np.ascontiguousarray(np.asarray(inputs["L0_attn_hidden_states"], dtype=np.float32))

    l0f_w = {"W1": inputs["L0_ffn_W1"], "b1": inputs["L0_ffn_b1"],
             "W2": inputs["L0_ffn_W2"], "b2": inputs["L0_ffn_b2"]}
    l1f_w = {"w1": inputs["L1_ffn_w1"], "b1": inputs["L1_ffn_b1"],
             "w2": inputs["L1_ffn_w2"], "b2": inputs["L1_ffn_b2"],
             "centroids": inputs["L1_ffn_centroids"], "w_route": inputs["L1_ffn_w_route"]}
    l2f_w = {"gate_weight": inputs["L2_ffn_gate_weight"], "gate_bias": inputs["L2_ffn_gate_bias"],
             "up_weights": inputs["L2_ffn_up_weights"], "down_weights": inputs["L2_ffn_down_weights"],
             "shared_up_weight": inputs["L2_ffn_shared_up_weight"],
             "shared_down_weight": inputs["L2_ffn_shared_down_weight"]}
    l3f_w = {"router_w": inputs["L3_ffn_router_w"], "w1": inputs["L3_ffn_w1"],
             "b1": inputs["L3_ffn_b1"], "w2": inputs["L3_ffn_w2"], "b2": inputs["L3_ffn_b2"]}

    x = _ffn_phase(x, "l0ffn", build_l0ffn, l0f_w)
    nc = _prog("l0a_og", build_l0attn_og)
    in_maps = []
    for c in range(N_CORES):
        b, hf = c // 2, c % 2
        sl = slice(hf * 512, (hf + 1) * 512)
        in_maps.append({
            "xfull": np.ascontiguousarray(x[b]),
            "Wq4": np.ascontiguousarray(inputs["L0_attn_Wq"][sl]),
            "Wk4": np.ascontiguousarray(inputs["L0_attn_Wk"][sl]),
            "Wv4": np.ascontiguousarray(inputs["L0_attn_Wv"][sl]),
            "Wg4": np.ascontiguousarray(inputs["L0_attn_Wg"][sl]),
            "lepe_w4": np.ascontiguousarray(np.asarray(inputs["L0_attn_lepe_w"])[sl, 0, :]),
            "lepe_b4": np.ascontiguousarray(inputs["L0_attn_lepe_b"][sl]),
        })
    ogs = _run(nc, in_maps, "og")
    import ml_dtypes
    ogT = np.zeros((B, D, S), dtype=ml_dtypes.bfloat16)
    for c in range(N_CORES):
        b, hf = c // 2, c % 2
        ogT[b, hf * 512:(hf + 1) * 512, :] = ogs[c]
    nc = _prog("l0a_out", build_l0attn_out)
    in_maps = []
    for c in range(N_CORES):
        b, hf = c // 2, c % 2
        in_maps.append({
            "ogT": np.ascontiguousarray(ogT[b, :, hf * T_LOC:(hf + 1) * T_LOC]),
            "Wo": inputs["L0_attn_Wo"],
            "x": np.ascontiguousarray(x[b, hf * T_LOC:(hf + 1) * T_LOC, :]),
        })
    x = _tok_gather(_run(nc, in_maps, "xo"), x)
    x = _ffn_phase(x, "l0ffn", build_l0ffn, l0f_w)

    x = _ffn_phase(x, "l1ffn", build_l1ffn, l1f_w)
    nc = _prog("l1attn", build_l1attn)
    in_maps = []
    for c in range(N_CORES):
        b, hf = c // 2, c % 2
        in_maps.append({
            "xfull": np.ascontiguousarray(x[b]),
            "xq": np.ascontiguousarray(x[b, hf * T_LOC:(hf + 1) * T_LOC, :]),
            "Wq": inputs["L1_attn_Wq"], "bq": inputs["L1_attn_bq"],
            "Wk": inputs["L1_attn_Wk"], "bk": inputs["L1_attn_bk"],
            "Wv": inputs["L1_attn_Wv"], "bv": inputs["L1_attn_bv"],
        })
    x = _tok_gather(_run(nc, in_maps, "xo"), x)
    x = _ffn_phase(x, "l1ffn", build_l1ffn, l1f_w)

    x = _ffn_phase(x, "l2ffn", build_l2ffn, l2f_w)
    nc = _prog("l2attn", build_l2attn)
    in_maps = []
    for c in range(N_CORES):
        b, hf = c // 2, c % 2
        in_maps.append({
            "xfull": np.ascontiguousarray(x[b]),
            "xq": np.ascontiguousarray(x[b, hf * T_LOC:(hf + 1) * T_LOC, :]),
            "Wq": inputs["L2_attn_Wq"], "Wk": inputs["L2_attn_Wk"], "Wv": inputs["L2_attn_Wv"],
            "Wproj": inputs["L2_attn_Wproj"], "bproj": inputs["L2_attn_bproj"],
            "Wpre": inputs["L2_attn_Wpre"], "bpre": inputs["L2_attn_bpre"],
            "pi": inputs["L2_attn_pi"],
        })
    x = _tok_gather(_run(nc, in_maps, "xo"), x)
    x = _ffn_phase(x, "l2ffn", build_l2ffn, l2f_w)

    x = _ffn_phase(x, "l3ffn", build_l3ffn, l3f_w)
    nc = _prog("l3gate", build_l3gate)
    shards = _tok_shards(x)
    xg_sh = _run(nc, [{"x": s, "Wg": inputs["L3_attn_Wg"], "bg": inputs["L3_attn_bg"]}
                      for s in shards], "xg")
    xg = _tok_gather(xg_sh, x)
    A2 = np.ascontiguousarray(np.asarray(inputs["L3_attn_A"]).reshape(D, 256))
    nc = _prog("l3ssm", build_l3ssm)
    in_maps = []
    for c in range(N_CORES):
        b, ch = c // 2, c % 2
        sl = slice(ch * 512, (ch + 1) * 512)
        in_maps.append({
            "xg": np.ascontiguousarray(xg[b, :, sl]),
            "A4": np.ascontiguousarray(A2[sl]),
            "Bp4": np.ascontiguousarray(inputs["L3_attn_Bp"][sl]),
            "Cp4": np.ascontiguousarray(inputs["L3_attn_Cp"][sl]),
            "Dp4": np.ascontiguousarray(inputs["L3_attn_Dp"][sl]),
        })
    yTs = _run(nc, in_maps, "yT")
    y = np.empty_like(x)
    for c in range(N_CORES):
        b, ch = c // 2, c % 2
        y[b, :, ch * 512:(ch + 1) * 512] = np.ascontiguousarray(yTs[c]).T
    nc = _prog("l3fin", build_l3fin)
    in_maps = []
    for c in range(N_CORES):
        b, hf = c // 2, c % 2
        in_maps.append({
            "y": np.ascontiguousarray(y[b, hf * T_LOC:(hf + 1) * T_LOC, :]),
            "x": np.ascontiguousarray(x[b, hf * T_LOC:(hf + 1) * T_LOC, :]),
            "gamma": inputs["L3_attn_gamma"], "beta": inputs["L3_attn_beta"],
        })
    x = _tok_gather(_run(nc, in_maps, "xo"), x)
    x = _ffn_phase(x, "l3ffn", build_l3ffn, l3f_w)
    return x
